# version 31 (final)
# baseline (speedup 1.0000x reference)
import contextlib
import math
import numpy as np
import concourse.bass as bass
import concourse.mybir as mybir
from concourse.bass_utils import run_bass_kernel_spmd

F32 = mybir.dt.float32
BF16 = mybir.dt.float16
I32 = mybir.dt.int32
ACT = mybir.ActivationFunctionType
ALU = mybir.AluOpType
AX = mybir.AxisListType

T = 2048
DM = 1024
NTT = T // 128
NCH = DM // 128
NH = 16
HD = 64
DR = 32
DKV = 256
L1SCALE = 1.0 / math.sqrt(96.0)
MAGIC32 = 1.5 * 2.0 ** 23
MAGIC64 = 1.5 * 2.0 ** 52

ENGS = ("tensor", "scalar", "vector", "gpsimd", "sync")
DEBUG = []


def host_consts():
    ident = np.eye(128, dtype=np.float32)
    tri = np.tril(np.ones((128, 128), dtype=bool))
    caus_add = np.where(tri, 0.0, -1e30).astype(np.float32)
    caus_mul = np.where(tri.T, 1.0, 0.0).astype(np.float32)
    rot = np.zeros((128, 128), dtype=np.float32)
    for blk in range(4):
        b0 = 32 * blk
        for d in range(16):
            rot[b0 + d + 16, b0 + d] = -1.0
            rot[b0 + d, b0 + d + 16] = 1.0
    half = np.float32(16.0)
    inv = (np.float32(1.0)
           / np.float32(10000.0) ** (np.arange(16, dtype=np.float32) / half))
    ang = (np.arange(T, dtype=np.float32)[:, None]
           * inv[None, :].astype(np.float32)).astype(np.float32)
    cos16 = np.cos(ang).astype(np.float32).T
    sin16 = np.sin(ang).astype(np.float32).T
    cosP = np.tile(np.concatenate([cos16, cos16], 0), (4, 1))
    sinP = np.tile(np.concatenate([sin16, sin16], 0), (4, 1))
    return {"ident": ident, "caus_add": caus_add, "caus_mul": caus_mul,
            "rot": rot, "cosP": cosP, "sinP": sinP}


class Reg:
    __slots__ = ("h", "w", "r", "name")

    def __init__(self, h, name):
        self.h = h
        self.name = name
        self.w = None
        self.r = []

    def __getitem__(self, idx):
        return self.h[idx]


class Win:
    __slots__ = ("t", "c0", "w")

    def __init__(self, t, c0, w):
        self.t, self.c0, self.w = t, c0, w

    def __getitem__(self, idx):
        if idx is Ellipsis:
            return self.t[:, self.c0:self.c0 + self.w]
        rows, cols = idx
        if isinstance(cols, slice):
            a = self.c0 + (cols.start or 0)
            bnd = self.c0 + (cols.stop if cols.stop is not None else self.w)
            return self.t[rows, a:bnd]
        raise TypeError(cols)


NLANE = 8
SEMKEYS = (("tensor", "scalar", "vector", "gpsimd")
           + tuple(f"sync_d{i}" for i in range(NLANE))
           + tuple(f"gpsimd_d{i}" for i in range(NLANE)))
ENG_OF = {k: ("sync" if k.startswith("sync_d") else
              "gpsimd" if k.startswith("gpsimd_d") else k) for k in SEMKEYS}


class B:
    def __init__(self, nc, stack):
        self.nc = nc
        self.stack = stack
        self.streams = {e: [] for e in ENGS}
        self.esem = {}
        for k in SEMKEYS:
            h = stack.enter_context(nc.semaphore("sem_" + k))
            self.esem[k] = [h, 0]
        self.hiwater = {e: {} for e in ENGS}
        self.drained = {e: 0 for e in ENGS}
        self.emitted = {e: 0 for e in ENGS}
        self.dma_idx = {"sync": 0, "gpsimd": 0}
        self.dbg_out = {}

    def sb(self, name, shape, dtype, scoped=None):
        st = scoped if scoped is not None else self.stack
        h = st.enter_context(self.nc.sbuf_tensor(name, shape, dtype))
        return Reg(h, name)

    def ps(self, name, shape, dtype=F32, scoped=None):
        st = scoped if scoped is not None else self.stack
        h = st.enter_context(self.nc.psum_tensor(name, shape, dtype))
        return Reg(h, name)

    def dreg(self, ap, name):
        return Reg(ap, name)

    BIGW = 74752

    def big(self, name, ncols):
        if not hasattr(self, "bigbuf"):
            self.bigbuf = self.stack.enter_context(
                self.nc.sbuf_tensor("BIG", [128, self.BIGW], BF16))
            self.big_free = [(0, self.BIGW)]
            self.big_deps = []
        ncols = (ncols + 15) // 16 * 16
        for i, (a, w) in enumerate(self.big_free):
            if w >= ncols:
                self.big_free[i] = (a + ncols, w - ncols)
                reg = Reg(Win(self.bigbuf, a, ncols), name)
                keep = []
                for (lo, hi, evs) in self.big_deps:
                    if lo < a + ncols and hi > a:
                        reg.r.extend(evs)
                    else:
                        keep.append((lo, hi, evs))
                self.big_deps = keep
                return reg
        raise MemoryError(f"big arena full allocating {name} ({ncols} cols); "
                          f"free={self.big_free}")

    def bigfree(self, *regs):
        for reg in regs:
            win = reg.h
            self.big_free.append((win.c0, win.w))
            evs = list(reg.r)
            if reg.w is not None:
                evs.append(reg.w)
            if evs:
                self.big_deps.append((win.c0, win.c0 + win.w, evs))
        self.big_free.sort()
        out = []
        for a, w in self.big_free:
            if out and out[-1][0] + out[-1][1] == a:
                out[-1] = (out[-1][0], out[-1][1] + w)
            else:
                out.append((a, w))
        self.big_free = [tuple(x) for x in out]

    def _wait(self, eng, srckey, val):
        if srckey == eng and eng in ("tensor", "scalar", "vector", "gpsimd"):
            if val > self.drained[eng]:
                self.streams[eng].append(lambda E: E.drain())
                self.drained[eng] = self.esem[eng][1]
        else:
            hw = self.hiwater[eng].get(srckey, 0)
            if val > hw:
                h = self.esem[srckey][0]
                self.streams[eng].append(lambda E, h=h, v=val: E.wait_ge(h, v))
                self.hiwater[eng][srckey] = val

    def _edge(self, eng, ev):
        if ev is not None:
            self._wait(eng, ev[0], ev[1])

    def op(self, eng, fn, reads=(), writes=(), acc=False, dma=False):
        if dma:
            assert eng in ("sync", "gpsimd")
            idx = self.dma_idx[eng]
            self.dma_idx[eng] = idx + 1
            key = f"{eng}_d{idx % NLANE}"
            if idx >= NLANE:
                self._wait(eng, key, 16 * (idx // NLANE))
        else:
            assert eng != "sync"
            key = eng
        for rg in reads:
            self._edge(eng, rg.w)
        for rg in writes:
            if not (acc and rg.w is not None and rg.w[0] == key):
                self._edge(eng, rg.w)
            for ev in rg.r:
                self._edge(eng, ev)
        sem = self.esem[key]
        k = 16 if dma else 1
        sem[1] += k
        n = sem[1]
        h = sem[0]

        def run(E, fn=fn, h=h, k=k):
            fn(E).then_inc(h, k)
        self.streams[eng].append(run)
        self.emitted[eng] += 1
        ev = (key, n)
        for rg in writes:
            rg.w = ev
            rg.r = []
        for rg in reads:
            rg.r.append(ev)
        return ev

    def fence(self, eng, ev):
        self._edge(eng, ev)

    def barrier(self):
        evs = []
        for e in ("tensor", "scalar", "vector", "gpsimd"):
            sem = self.esem[e]
            sem[1] += 1
            h, n = sem[0], sem[1]
            self.streams[e].append(lambda E, h=h: E.drain().then_inc(h, 1))
            self.emitted[e] += 1
            evs.append((e, n))
        for i in range(NLANE):
            evs.append((f"sync_d{i}", self.esem[f"sync_d{i}"][1]))
            evs.append((f"gpsimd_d{i}", self.esem[f"gpsimd_d{i}"][1]))
        for e in ENGS:
            for ev in evs:
                if ev[0] != e:
                    self._wait(e, ev[0], ev[1])
            if e in self.esem:
                self.drained[e] = self.esem[e][1]
        return evs

    def finalize(self):
        with self.nc.Block() as block:
            for eng in ENGS:
                dec = getattr(block, eng)
                lst = self.streams[eng]

                def make(lst_):
                    def run(E):
                        for fn in lst_:
                            fn(E)
                    return run
                dec(make(lst))

    def dump(self, name, reg, ap=None):
        if name not in DEBUG:
            return
        a = reg[...] if ap is None else ap
        shp = list(a.shape)
        d = self.nc.dram_tensor("dbg_" + name, shp, a.dtype,
                                kind="ExternalOutput").ap()
        self.op("sync", lambda e, d=d, a=a: e.dma_start(out=d, in_=a),
                reads=[reg], dma=True)
        self.dbg_out[name] = shp


def build_nc(stack):
    nc = bass.Bass("TRN2", target_bir_lowering=False, debug=False)
    b = B(nc, stack)

    DN = {}

    def din(name, shape, dtype=F32):
        DN[name] = nc.dram_tensor(name, shape, dtype, kind="ExternalInput").ap()
        return DN[name]

    x_in = din("x", [T, DM])
    W_qkv = din("W_qkv", [DM, 3 * DM])
    b_qkv = din("b_qkv", [3 * DM])
    W_proj = din("W_proj", [DM, DM])
    b_proj = din("b_proj", [DM])
    tern = din("tern", [DM, DM], I32)
    fbias = din("fbias", [DM])
    ascale = din("ascale", [1])
    Wq = din("Wq", [DM, DM])
    Wqr = din("Wqr", [DM, 512])
    Wkr = din("Wkr", [DM, DR])
    Wkvd = din("Wkvd", [DM, DKV])
    Wku = din("Wku", [DKV, DM])
    Wvu = din("Wvu", [DKV, DM])
    Wo = din("Wo", [DM, DM])
    lobo = din("lobo", [NH])
    Wr = din("Wr", [DM, DM])
    Wk = din("Wk", [DM, DM])
    Wv = din("Wv", [DM, DM])
    mu_r = din("mu_r", [DM])
    mu_k = din("mu_k", [DM])
    for nm in ("ident", "caus_add", "caus_mul", "rot"):
        din(nm, [128, 128])
    din("cosP", [128, T])
    din("sinP", [128, T])

    y_out = nc.dram_tensor("y", [T, DM], F32, kind="ExternalOutput").ap()
    g0_dram = b.dreg(nc.dram_tensor("g0_s", [T, DM], F32, kind="Internal").ap(), "g0d")
    x1_dram = b.dreg(nc.dram_tensor("x1_s", [T, DM], F32, kind="Internal").ap(), "x1d")
    wT_dram = b.dreg(nc.dram_tensor("wT_s", [3, DM, DM], BF16, kind="Internal").ap(), "wTd")
    y_reg = b.dreg(y_out, "y")

    ident_f = b.sb("ident_f", [128, 128], F32)
    caus_add_s = b.sb("caus_add_s", [128, 128], F32)
    caus_mul_s = b.sb("caus_mul_s", [128, 128], BF16)
    rot_s = b.sb("rot_s", [128, 128], BF16)
    ones_bf = b.sb("ones_bf", [1, 128], BF16)
    ones_f = b.sb("ones_f", [1, 128], F32)
    sel64 = b.sb("sel64", [128, 2], BF16)
    sel_lo = b.sb("sel_lo", [1, 128], BF16)
    onescol_f = b.sb("onescol_f", [128, 1], F32)
    xnbrow = b.sb("xnbrow", [1, DM], F32)
    sel_hi = b.sb("sel_hi", [1, 128], BF16)
    bq_col = b.sb("bq_col", [128, 24], F32)
    bp_row = b.sb("bp_row", [1, DM], F32)
    fbs_row = b.sb("fbs_row", [1, DM], F32)
    bv_row = b.sb("bv_row", [1, DM], F32)
    lobo_row = b.sb("lobo_row", [1, NH], F32)
    mu_col = b.sb("mu_col", [128, 16], F32)
    smal = b.sb("smal", [128, 16], F32)
    C_SCALE, C_INVS, C_ALPHA, C_E, C_075, C_AS, C_SC1, C_IS1 = range(8)

    psA = b.ps("psA", [128, 2048], F32)
    psB = b.ps("psB", [128, 2048], F32)

    sync_dma = lambda dst, src: (lambda e, d=dst, s=src: e.dma_start(out=d, in_=s))
    gp_dma = sync_dma

    for dst, srcn in ((ident_f, "ident"), (caus_add_s, "caus_add")):
        b.op("sync", sync_dma(dst[...], DN[srcn]), writes=[dst], dma=True)
    for dst, srcn in ((caus_mul_s, "caus_mul"), (rot_s, "rot")):
        b.op("gpsimd", gp_dma(dst[...], DN[srcn]), writes=[dst], dma=True)
    p0 = stack.enter_context(contextlib.ExitStack())
    rows_tmp = b.sb("rows_tmp", [1, 3072 + 2048], F32, scoped=p0)
    fb_row = b.sb("fb_row", [1, DM], F32, scoped=p0)
    b.op("sync", sync_dma(rows_tmp[0:1, 0:3072],
                          b_qkv.rearrange("(a n) -> a n", a=1)),
         writes=[rows_tmp], dma=True)
    b.op("sync", sync_dma(rows_tmp[0:1, 3072:4096],
                          mu_r.rearrange("(a n) -> a n", a=1)),
         writes=[rows_tmp], dma=True)
    b.op("sync", sync_dma(rows_tmp[0:1, 4096:5120],
                          mu_k.rearrange("(a n) -> a n", a=1)),
         writes=[rows_tmp], dma=True)
    b.op("sync", sync_dma(fb_row[...], fbias.rearrange("(a n) -> a n", a=1)),
         writes=[fb_row], dma=True)
    b.op("sync", sync_dma(lobo_row[...], lobo.rearrange("(a n) -> a n", a=1)),
         writes=[lobo_row], dma=True)
    b.op("sync", sync_dma(smal[0:1, C_AS:C_AS + 1], ascale.rearrange("(a n) -> a n", a=1)),
         writes=[smal], dma=True)
    for c in range(24):
        b.op("tensor", lambda e, c=c: e.transpose(
            psB[:, c:c + 1], rows_tmp[0:1, c * 128:(c + 1) * 128],
            ident_f[0:1, 0:1]),
            reads=[rows_tmp, ident_f], writes=[psB], acc=True)
    for c in range(16):
        b.op("tensor", lambda e, c=c: e.transpose(
            psB[:, 24 + c:24 + c + 1],
            rows_tmp[0:1, 3072 + c * 128:3072 + (c + 1) * 128],
            ident_f[0:1, 0:1]),
            reads=[rows_tmp, ident_f], writes=[psB], acc=True)
    b.op("vector", lambda e: e.tensor_copy(bq_col[...], psB[:, 0:24]),
         reads=[psB], writes=[bq_col])
    b.op("vector", lambda e: e.tensor_copy(mu_col[...], psB[:, 24:40]),
         reads=[psB], writes=[mu_col])
    b.op("sync", sync_dma(bp_row[...], b_proj.rearrange("(a n) -> a n", a=1)),
         writes=[bp_row], dma=True)
    b.op("sync", sync_dma(bv_row[...], b_qkv.rearrange("(a n) -> a n", a=1)[:, 2048:3072]),
         writes=[bv_row], dma=True)
    b.op("gpsimd", lambda e: e.memset(ones_bf[...], 1.0), writes=[ones_bf])
    b.op("gpsimd", lambda e: e.memset(ones_f[...], 1.0), writes=[ones_f])
    b.op("gpsimd", lambda e: e.memset(sel64[...], 0.0), writes=[sel64])
    b.op("gpsimd", lambda e: e.memset(sel_lo[...], 0.0), writes=[sel_lo])
    b.op("gpsimd", lambda e: e.memset(sel_lo[0:1, 0:64], 1.0), writes=[sel_lo])
    b.op("gpsimd", lambda e: e.memset(sel_hi[...], 0.0), writes=[sel_hi])
    b.op("gpsimd", lambda e: e.memset(onescol_f[...], 1.0), writes=[onescol_f])
    b.op("gpsimd", lambda e: e.memset(sel_hi[0:1, 64:128], 1.0), writes=[sel_hi])
    b.op("gpsimd", lambda e: e.memset(sel64[0:64, 0:1], 1.0), writes=[sel64])
    b.op("gpsimd", lambda e: e.memset(sel64[64:128, 1:2], 1.0), writes=[sel64])
    b.op("gpsimd", lambda e: e.memset(smal[:, C_075:C_075 + 1], 0.75),
         writes=[smal])
    for col, mg in ((8, MAGIC32), (9, -MAGIC32), (10, MAGIC64), (11, -MAGIC64)):
        b.op("gpsimd", lambda e, col=col, mg=mg: e.memset(
            smal[:, col:col + 1], mg), writes=[smal])
        nc.const_aps.register(F32, mg, smal.h[:, col:col + 1])
    b.op("vector", lambda e: e.tensor_scalar(
        smal[:, C_E:C_E + 1], smal[:, C_075:C_075 + 1], MAGIC32, -MAGIC32,
        ALU.add, ALU.add), reads=[smal], writes=[smal])
    b.op("vector", lambda e: e.tensor_scalar(
        smal[:, C_ALPHA:C_ALPHA + 1], smal[:, C_E:C_E + 1], -0.75, 4.0,
        ALU.add, ALU.mult), reads=[smal], writes=[smal])
    b.op("vector", lambda e: e.tensor_scalar(
        smal[0:1, C_SC1:C_SC1 + 1], smal[0:1, C_AS:C_AS + 1], 1e-5, None,
        ALU.max), reads=[smal], writes=[smal])
    b.op("vector", lambda e: e.reciprocal(
        smal[0:1, C_IS1:C_IS1 + 1], smal[0:1, C_SC1:C_SC1 + 1]),
        reads=[smal], writes=[smal])
    b.op("vector", lambda e: e.tensor_scalar(
        fbs_row[...], fb_row[...], smal[0:1, C_IS1:C_IS1 + 1], None, ALU.mult),
        reads=[fb_row, smal], writes=[fbs_row])
    b.op("tensor", lambda e: e.matmul(
        psA[:, 0:2], ones_f[...], smal[0:1, C_SC1:C_IS1 + 1], start=True,
        stop=True), reads=[ones_f, smal], writes=[psA])
    b.op("vector", lambda e: e.tensor_copy(smal[:, C_SCALE:C_INVS + 1],
                                           psA[:, 0:2]),
         reads=[psA], writes=[smal])
    b.barrier()
    p0.close()

    w0T = [b.big(f"w0T_{c}", DM) for c in range(NCH)]
    p1 = stack.enter_context(contextlib.ExitStack())
    tmpi = [b.sb(f"p1i_{i}", [128, DM], I32, scoped=p1) for i in range(2)]
    tmpf = [b.sb(f"p1f_{i}", [128, DM], F32, scoped=p1) for i in range(2)]
    wtt2 = [b.sb(f"p1wt{i}", [128, DM], BF16, scoped=p1) for i in range(2)]
    wttV2 = [[Reg(Win(w.h, cc * 128, 128), f"wtt{i}_{cc}") for cc in range(NCH)]
             for i, w in enumerate(wtt2)]

    for oc in range(NCH):
        ti, tf = tmpi[oc % 2], tmpf[oc % 2]
        b.op("gpsimd", gp_dma(ti[...], tern[oc * 128:(oc + 1) * 128, :]),
             writes=[ti], dma=True)
        b.op("vector", lambda e, ti=ti, tf=tf: e.tensor_scalar(
            tf[...], ti[...], -1.0, None, ALU.add), reads=[ti], writes=[tf])
        ps = psA if oc % 2 == 0 else psB
        for ic in range(NCH):
            b.op("tensor", lambda e, tf=tf, ic=ic, pm=ps: e.transpose(
                pm[:, ic * 128:ic * 128 + 128],
                tf[:, ic * 128:(ic + 1) * 128], ident_f[...]),
                reads=[tf, ident_f], writes=[ps], acc=True)
        for ic in range(NCH):
            b.op("vector", lambda e, ic=ic, oc=oc, pm=ps: e.tensor_copy(
                w0T[ic][:, oc * 128:(oc + 1) * 128],
                pm[:, ic * 128:ic * 128 + 128]),
                reads=[ps], writes=[w0T[ic]])
    for wi, wsrc in enumerate((Wr, Wk, Wv)):
        for dc in range(NCH):
            tf = tmpf[dc % 2]
            b.op("gpsimd", gp_dma(tf[...], wsrc[dc * 128:(dc + 1) * 128, :]),
                 writes=[tf], dma=True)
            ps = psA if dc % 2 == 0 else psB
            for cc in range(NCH):
                b.op("tensor", lambda e, tf=tf, cc=cc, pm=ps: e.transpose(
                    pm[:, cc * 128:cc * 128 + 128],
                    tf[:, cc * 128:(cc + 1) * 128], ident_f[...]),
                    reads=[tf, ident_f], writes=[ps], acc=True)
            wtt = wtt2[dc % 2]
            wttV = wttV2[dc % 2]
            for cc in range(NCH):
                b.op("scalar", lambda e, cc=cc, pm=ps, wtt=wtt: e.activation(
                    wtt[:, cc * 128:(cc + 1) * 128],
                    pm[:, cc * 128:cc * 128 + 128], ACT.Copy),
                    reads=[ps], writes=[wttV[cc]])
            b.op("sync", lambda e, wi=wi, dc=dc, wtt=wtt: e.dma_start(
                out=wT_dram[...][wi].rearrange("(a p) n -> p a n", p=128)
                [:, :, dc * 128:(dc + 1) * 128],
                in_=wtt[...].rearrange("p (a c) -> p a c", c=128)),
                reads=wttV, writes=[wT_dram], dma=True)
    b.barrier()
    p1.close()

    xnT = [b.big(f"xnT_{c}", T) for c in range(NCH)]
    xiT = [b.big(f"xiT_{c}", T) for c in range(NCH)]
    p2 = stack.enter_context(contextlib.ExitStack())
    x0t = [b.sb(f"x0t_{i}", [128, DM], F32, scoped=p2) for i in range(2)]
    vq = [b.sb(f"vq_{i}", [128, DM], F32, scoped=p2) for i in range(2)]
    sgn2 = [b.sb(f"sgn_{i}", [128, DM], F32, scoped=p2) for i in range(2)]
    rq22 = [b.sb(f"rq2_{i}", [128, DM], F32, scoped=p2) for i in range(2)]
    xn_f = [b.sb("xn_f_0", [128, DM], F32, scoped=p2)] * 2
    statw = b.sb("statw", [128, 112], F32, scoped=p2)
    sumR = [Reg(Win(statw.h, t, 1), f"su{t}") for t in range(NTT)]
    sqR = [Reg(Win(statw.h, NTT + t, 1), f"sq{t}") for t in range(NTT)]
    C_SU, C_SQ, C_M, C_V, C_R, C_T, C_7 = 0, 16, 32, 48, 64, 80, 96
    stw = Reg(Win(statw.h, 32, 80), "stw")

    for tt in range(NTT):
        xb = x0t[tt % 2]
        sq = vq[tt % 2]
        b.op("sync", sync_dma(xb[...], x_in[tt * 128:(tt + 1) * 128, :]),
             writes=[xb], dma=True)
        b.op("scalar", lambda e, xb=xb, sq=sq: e.activation(
            sq[...], xb[...], ACT.Square), reads=[xb], writes=[sq])
        b.op("vector", lambda e, xb=xb, tt=tt: e.reduce_sum(
            statw[:, tt:tt + 1], xb[...], axis=AX.X),
            reads=[xb], writes=[sumR[tt]])
        b.op("vector", lambda e, tt=tt, sq=sq: e.reduce_sum(
            statw[:, NTT + tt:NTT + tt + 1], sq[...], axis=AX.X),
            reads=[sq], writes=[sqR[tt]])

    def sc(dst, src, s1, s2, o1, o2=None):
        def _f(e):
            if o2 is None:
                return e.tensor_scalar(statw[:, dst:dst + 16],
                                       statw[:, src:src + 16], s1, s2, o1)
            return e.tensor_scalar(statw[:, dst:dst + 16],
                                   statw[:, src:src + 16], s1, s2, o1, o2)
        b.op("vector", _f, reads=[stw] + sumR + sqR, writes=[stw])

    def tc(dst, a, c2, o):
        b.op("vector", lambda e: e.tensor_tensor(
            statw[:, dst:dst + 16], statw[:, a:a + 16], statw[:, c2:c2 + 16],
            o), reads=[stw], writes=[stw])
    sc(C_M, C_SU, 1.0 / DM, None, ALU.mult)
    tc(C_T, C_M, C_M, ALU.mult)
    sc(C_V, C_SQ, 1.0 / DM, None, ALU.mult)
    tc(C_V, C_V, C_T, ALU.subtract)
    sc(C_V, C_V, 1e-5, None, ALU.add)
    b.op("vector", lambda e: e.reciprocal(
        statw[:, C_T:C_T + 16], statw[:, C_V:C_V + 16]),
        reads=[stw], writes=[stw])
    b.op("scalar", lambda e: e.activation(
        statw[:, C_R:C_R + 16], statw[:, C_T:C_T + 16], ACT.Sqrt),
        reads=[stw], writes=[stw])
    tc(C_T, C_R, C_R, ALU.mult)
    tc(C_T, C_T, C_V, ALU.mult)
    sc(C_T, C_T, -0.5, 1.5, ALU.mult, ALU.add)
    tc(C_R, C_R, C_T, ALU.mult)
    b.op("vector", lambda e: e.tensor_scalar(
        statw[:, C_7:C_7 + 16], statw[:, C_R:C_R + 16],
        smal[:, C_INVS:C_INVS + 1], None, ALU.mult),
        reads=[stw, smal], writes=[stw])

    pA0 = Reg(Win(psA.h, 0, 1024), "pA0")
    pB0 = Reg(Win(psB.h, 0, 1024), "pB0")
    pB1 = Reg(Win(psB.h, 1024, 1024), "pB1")
    for v, src in ((pA0, psA), (pB0, psB), (pB1, psB)):
        v.w = src.w
        v.r = list(src.r)
    for tt in range(NTT):
        xb = x0t[tt % 2]
        xn = xn_f[tt % 2]
        vb = vq[tt % 2]
        sgn = sgn2[tt % 2]
        rq2 = rq22[tt % 2]
        b.op("sync", sync_dma(xb[...], x_in[tt * 128:(tt + 1) * 128, :]),
             writes=[xb], dma=True)
        b.op("vector", lambda e, xb=xb, xn=xn, tt=tt: e.tensor_scalar(
            xn[...], xb[...], statw[:, C_M + tt:C_M + tt + 1],
            statw[:, C_R + tt:C_R + tt + 1],
            ALU.subtract, ALU.mult), reads=[xb, stw], writes=[xn])
        b.op("vector", lambda e, xb=xb, vb=vb, tt=tt: e.tensor_scalar(
            vb[...], xb[...], statw[:, C_M + tt:C_M + tt + 1],
            statw[:, C_7 + tt:C_7 + tt + 1],
            ALU.subtract, ALU.mult), reads=[xb, stw], writes=[vb])
        b.op("vector", lambda e, vb=vb, sgn=sgn: e.tensor_scalar(
            sgn[...], vb[...], MAGIC32, -MAGIC32, ALU.add, ALU.add),
            reads=[vb], writes=[sgn])
        b.op("scalar", lambda e, vb=vb, rq2=rq2: e.add(
            rq2[...], vb[...], MAGIC64), reads=[vb, smal], writes=[rq2])
        b.op("scalar", lambda e, rq2=rq2: e.add(
            rq2[...], rq2[...], -MAGIC64), reads=[smal], writes=[rq2])
        b.op("vector", lambda e, sgn=sgn, rq2=rq2: e.tensor_tensor(
            sgn[...], sgn[...], rq2[...], ALU.subtract),
            reads=[sgn, rq2], writes=[sgn])
        b.op("vector", lambda e, sgn=sgn, rq2=rq2: e.scalar_tensor_tensor(
            sgn[...], sgn[...], smal[:, C_ALPHA:C_ALPHA + 1], rq2[...],
            ALU.mult, ALU.add), reads=[sgn, smal, rq2], writes=[sgn])
        b.op("vector", lambda e, vb=vb, sgn=sgn: e.tensor_scalar(
            vb[...], sgn[...], 127.0, -127.0, ALU.min, ALU.max),
            reads=[sgn], writes=[vb])
        for fc in range(2):
            b.op("tensor", lambda e, xn=xn, fc=fc, tt=tt: e.matmul(
                psB[0:1, 1024 + fc * 512:1024 + (fc + 1) * 512], onescol_f[...],
                xn[:, fc * 512:(fc + 1) * 512], start=(tt == 0), stop=False,
                skip_group_check=True),
                reads=[onescol_f, xn], writes=[pB1], acc=True)
        for c in range(NCH):
            b.op("tensor", lambda e, c=c, xn=xn: e.transpose(
                psA[:, c * 128:c * 128 + 128], xn[:, c * 128:(c + 1) * 128],
                ident_f[...]), reads=[xn, ident_f], writes=[pA0], acc=True)
            b.op("tensor", lambda e, c=c, vb=vb: e.transpose(
                psB[:, c * 128:c * 128 + 128], vb[:, c * 128:(c + 1) * 128],
                ident_f[...]), reads=[vb, ident_f], writes=[pB0], acc=True)
        for c in range(NCH):
            b.op("vector", lambda e, c=c, tt=tt: e.tensor_copy(
                xnT[c][:, tt * 128:(tt + 1) * 128],
                psA[:, c * 128:c * 128 + 128]), reads=[pA0], writes=[xnT[c]])
            b.op("scalar", lambda e, c=c, tt=tt: e.activation(
                xiT[c][:, tt * 128:(tt + 1) * 128],
                psB[:, c * 128:c * 128 + 128], ACT.Copy),
                reads=[pB0], writes=[xiT[c]])
    b.op("vector", lambda e: e.tensor_scalar(
        xnbrow[...], psB[0:1, 1024:2048], 1.0 / T, None, ALU.mult),
        reads=[pB1], writes=[xnbrow])
    b.barrier()
    p2.close()
    for c in range(NCH):
        b.dump(f"xnT{c}", xnT[c])
        b.dump(f"xiT{c}", xiT[c])

    p3 = stack.enter_context(contextlib.ExitStack())
    g0t = [b.sb(f"g0t_{i}", [128, DM], F32, scoped=p3) for i in range(2)]
    for tt in range(NTT):
        ps = psA if tt % 2 == 0 else psB
        gb = g0t[tt % 2]
        for fc in range(2):
            for c in range(NCH):
                b.op("tensor", lambda e, c=c, fc=fc, tt=tt, pm=ps: e.matmul(
                    pm[:, fc * 512:(fc + 1) * 512],
                    xiT[c][:, tt * 128:(tt + 1) * 128],
                    w0T[c][:, fc * 512:(fc + 1) * 512],
                    start=(c == 0), stop=False),
                    reads=[xiT[c], w0T[c]], writes=[ps], acc=True)
            b.op("tensor", lambda e, fc=fc, pm=ps: e.matmul(
                pm[:, fc * 512:(fc + 1) * 512], ones_f[...],
                fbs_row[0:1, fc * 512:(fc + 1) * 512], start=False, stop=True),
                reads=[ones_f, fbs_row], writes=[ps], acc=True)
        b.op("scalar", lambda e, gb=gb, pm=ps: e.activation(
            gb[...], pm[:, 0:1024], ACT.Silu, scale=smal[:, C_SCALE:C_SCALE + 1]),
            reads=[ps, smal], writes=[gb])
        b.op("sync", lambda e, gb=gb, tt=tt: e.dma_start(
            out=g0_dram[...][tt * 128:(tt + 1) * 128, :], in_=gb[...]),
            reads=[gb], writes=[g0_dram], dma=True)
    b.barrier()
    p3.close()
    b.bigfree(*xiT, *w0T)
    b.dump("g0", g0_dram)

    qT = [b.big(f"qT_{c}", T) for c in range(NCH)]
    kT = [b.big(f"kT_{c}", T) for c in range(NCH)]
    vtok = [b.big(f"vtok_{t}", NH * 65) for t in range(NTT)]
    EX = b.sb("EX", [128, NH], F32)
    vbp = b.sb("vbp", [1, DM], F32)
    vAcol = b.sb("vAcol", [128, 8], F32)
    negvA = b.sb("negvA", [128, 8], F32)
    p4 = stack.enter_context(contextlib.ExitStack())
    wqsl = [b.sb(f"wqsl_{i}", [128, DM], BF16, scoped=p4) for i in range(2)]
    xnb = b.sb("xnb", [128, 8], F32, scoped=p4)
    qsq = b.sb("qsq", [128, T], BF16, scoped=p4)
    fro82 = b.sb("fro82", [2, NCH], F32, scoped=p4)
    infc = b.sb("infc", [128, NCH + 2], F32, scoped=p4)
    ehrow = b.sb("ehrow", [1, NH], F32, scoped=p4)
    ones16 = b.sb("ones16", [128, 1], BF16, scoped=p4)
    urow = b.sb("urow", [1, 1024], F32, scoped=p4)
    RU = b.big("RU", T)
    OT = [None] * NCH
    psS = Reg(Win(psA.h, 0, 1024), "psS")
    psQ = Reg(Win(psA.h, 1024, 1024), "psQ")

    p4v = contextlib.ExitStack()
    wvsl = [b.sb(f"wvsl_{i}", [128, 4096], BF16, scoped=p4v) for i in range(2)]
    wvf = b.sb("wvf", [128, 1024], F32, scoped=p4v)
    b.op("gpsimd", lambda e: e.memset(ones16[...], 1.0), writes=[ones16])
    for t in range(NTT):
        b.op("gpsimd", lambda e, t=t: e.memset(
            vtok[t][...].rearrange("p (h j) -> p h j", j=65)[:, :, 64:65], 1.0),
            writes=[vtok[t]])
    for c in range(NCH):
        b.op("tensor", lambda e, c=c: e.transpose(
            psQ[0:128, 512 + c:512 + c + 1], xnbrow[0:1, c * 128:(c + 1) * 128],
            ident_f[0:1, 0:1]), reads=[xnbrow, ident_f], writes=[psQ], acc=True)
    b.op("vector", lambda e: e.tensor_copy(xnb[...], psQ[0:128, 512:520]),
         reads=[psQ], writes=[xnb])
    for fc in range(2):
        wb = wvsl[fc]
        b.op("gpsimd", lambda e, wb=wb, fc=fc: e.dma_start(
            out=wb[...].rearrange("p (a n) -> p a n", n=512),
            in_=W_qkv.rearrange("(a p) n -> p a n", p=128)
            [:, :, 2048 + fc * 512:2048 + (fc + 1) * 512]),
            writes=[wb], dma=True)
        for hf in range(4):
            b.op("sync", lambda e, fc=fc, hf=hf: e.dma_start(
                out=wvf[...].rearrange("p (a n) -> p a n", n=512),
                in_=W_qkv.rearrange("(a p) n -> p a n", p=128)
                [:, 2 * hf:2 * hf + 2, 2048 + fc * 512:2048 + (fc + 1) * 512]),
                writes=[wvf], dma=True)
            for a in range(2):
                b.op("tensor", lambda e, fc=fc, hf=hf, a=a: e.matmul(
                    psB[32:33, 1024 + fc * 512:1024 + (fc + 1) * 512],
                    xnb[:, 2 * hf + a:2 * hf + a + 1],
                    wvf[:, a * 512:(a + 1) * 512],
                    start=(hf == 0 and a == 0), stop=False,
                    skip_group_check=True),
                    reads=[xnb, wvf], writes=[psB], acc=True)
    for tt in range(NTT):
        ps = psA if tt % 2 == 0 else psB
        for fc in range(2):
            for dc in range(NCH):
                b.op("tensor", lambda e, fc=fc, dc=dc, tt=tt, pm=ps: e.matmul(
                    pm[:, fc * 512:(fc + 1) * 512],
                    xnT[dc][:, tt * 128:(tt + 1) * 128],
                    wvsl[fc][:, dc * 512:(dc + 1) * 512],
                    start=(dc == 0), stop=False),
                    reads=[xnT[dc], wvsl[fc]], writes=[ps], acc=True)
            b.op("tensor", lambda e, fc=fc, pm=ps: e.matmul(
                pm[:, fc * 512:(fc + 1) * 512], ones_f[...],
                bv_row[0:1, fc * 512:(fc + 1) * 512], start=False, stop=True),
                reads=[ones_f, bv_row], writes=[ps], acc=True)
        b.op("scalar", lambda e, tt=tt, pm=ps: e.activation(
            vtok[tt][...].rearrange("p (h j) -> p h j", j=65)[:, :, 0:64],
            pm[:, 0:1024].rearrange("p (h j) -> p h j", j=64), ACT.Copy),
            reads=[ps], writes=[vtok[tt]])
    b.op("vector", lambda e: e.tensor_tensor(
        vbp[...], psB[32:33, 1024:2048], bv_row[...], ALU.add),
        reads=[psB, bv_row], writes=[vbp])
    for tt in range(NTT):
        for (a0, a1) in ((0, 512), (512, 1024), (1024, 1040)):
            b.op("tensor", lambda e, tt=tt, a0=a0, a1=a1: e.matmul(
                psB[0:1, a0:a1], ones16[...], vtok[tt][:, a0:a1],
                start=(tt == 0), stop=(tt == NTT - 1), skip_group_check=True),
                reads=[ones16, vtok[tt]], writes=[psB], acc=True)
    b.op("vector", lambda e: e.tensor_scalar(
        qsq[0:1, 0:1040], psB[0:1, 0:1040], 1.0 / T, None, ALU.mult),
        reads=[psB], writes=[qsq])
    for c in range(NCH):
        for half in range(2):
            h2 = 2 * c + half
            b.op("vector", lambda e, half=half, h2=h2: e.tensor_copy(
                urow[0:1, 64 * half:64 * half + 64],
                qsq[0:1, 65 * h2:65 * h2 + 64]),
                reads=[qsq], writes=[urow])
        b.op("tensor", lambda e, c=c: e.transpose(
            psQ[0:128, 520 + c:520 + c + 1], urow[0:1, 0:128], ident_f[0:1, 0:1]),
            reads=[urow, ident_f], writes=[psQ], acc=True)
    b.op("vector", lambda e: e.tensor_copy(vAcol[...], psQ[0:128, 520:528]),
         reads=[psQ], writes=[vAcol])
    b.op("vector", lambda e: e.tensor_scalar(
        negvA[...], vAcol[...], -1.0, None, ALU.mult),
        reads=[vAcol], writes=[negvA])
    for c in range(NCH):
        b.op("scalar", lambda e, c=c: e.activation(
            qsq[...], xnT[c][...], ACT.Abs), reads=[xnT[c]], writes=[qsq])
        b.op("vector", lambda e, c=c: e.reduce_sum(
            infc[:, c:c + 1], qsq[...], axis=AX.X), reads=[qsq], writes=[infc])
    b.op("vector", lambda e: e.reduce_max(
        infc[:, 8:9], infc[:, 0:8], axis=AX.X), reads=[infc], writes=[infc])
    b.op("tensor", lambda e: e.transpose(
        psQ[0:1, 528:656], infc[:, 8:9], ident_f[...]),
        reads=[infc, ident_f], writes=[psQ])
    b.op("vector", lambda e: e.reduce_max(
        infc[0:1, 9:10], psQ[0:1, 528:656], axis=AX.X),
        reads=[psQ], writes=[infc])
    p4v.close()
    Pexp = b.sb("Pexp", [128, 4096], F32, scoped=p4)
    drec = b.sb("drec", [1, 1024], BF16, scoped=p4)
    urow16 = b.sb("urow16", [1, 1024], BF16, scoped=p4)
    lnrow = b.sb("lnrow", [1, 1024], F32, scoped=p4)

    def qkv_pair_thunks(p):
        th = []
        for which in range(2):
            dst = qT[p] if which == 0 else kT[p]
            col0 = p * 128 if which == 0 else 1024 + p * 128
            bcol = p if which == 0 else 8 + p
            wb = wqsl[(2 * p + which) % 2]
            th.append(lambda wb=wb, col0=col0: b.op(
                "gpsimd", lambda e: e.dma_start(
                    out=wb[...].rearrange("p (a n) -> p a n", n=128),
                    in_=W_qkv.rearrange("(a p) n -> p a n", p=128)
                    [:, :, col0:col0 + 128]), writes=[wb], dma=True))
            for half in range(4):
                lo = (half % 2) * 512
                for dc in range(NCH):
                    th.append(lambda wb=wb, dc=dc, half=half, lo=lo: b.op(
                        "tensor", lambda e: e.matmul(
                            psQ[0:128, lo:lo + 512],
                            wb[:, dc * 128:(dc + 1) * 128],
                            xnT[dc][:, half * 512:(half + 1) * 512],
                            start=(dc == 0), stop=(dc == NCH - 1)),
                        reads=[wb, xnT[dc]], writes=[psQ], acc=True))
                th.append(lambda dst=dst, half=half, bcol=bcol, lo=lo: b.op(
                    "vector", lambda e: e.tensor_scalar(
                        dst[0:128, half * 512:(half + 1) * 512],
                        psQ[0:128, lo:lo + 512],
                        bq_col[:, bcol:bcol + 1], None, ALU.add),
                    reads=[psQ, bq_col], writes=[dst]))
        th.append(lambda p=p: b.op("scalar", lambda e: e.activation(
            qsq[...], qT[p][...], ACT.Square), reads=[qT[p]], writes=[qsq]))
        for qs in range(4):
            th.append(lambda qs=qs: b.op("tensor", lambda e: e.matmul(
                psQ[0:2, 0:512], sel64[...], qsq[:, qs * 512:(qs + 1) * 512],
                start=(qs == 0), stop=(qs == 3), skip_group_check=True),
                reads=[sel64, qsq], writes=[psQ], acc=True))
        th.append(lambda p=p: b.op("vector", lambda e: e.reduce_sum(
            fro82[0:2, p:p + 1], psQ[0:2, 0:512], axis=AX.X),
            reads=[psQ], writes=[fro82]))
        th.append(lambda p=p: b.op("tensor", lambda e: e.transpose(
            psQ[0:1, 512:514], fro82[0:2, p:p + 1],
            ident_f[0:2, 0:2]), reads=[fro82, ident_f], writes=[psQ], acc=True))
        th.append(lambda p=p: b.op("scalar", lambda e: e.activation(
            ehrow[0:1, 2 * p:2 * p + 2], psQ[0:1, 512:514],
            ACT.Sqrt), reads=[psQ], writes=[ehrow]))
        th.append(lambda p=p: b.op("vector", lambda e: e.tensor_scalar(
            ehrow[0:1, 2 * p:2 * p + 2], ehrow[0:1, 2 * p:2 * p + 2],
            infc[0:1, 9:10], None, ALU.mult), reads=[ehrow, infc], writes=[ehrow]))
        th.append(lambda p=p: b.op("vector", lambda e: e.reciprocal(
            ehrow[0:1, 2 * p:2 * p + 2], ehrow[0:1, 2 * p:2 * p + 2]),
            reads=[ehrow], writes=[ehrow]))
        th.append(lambda p=p: b.op("vector", lambda e: e.tensor_scalar(
            ehrow[0:1, 2 * p:2 * p + 2], ehrow[0:1, 2 * p:2 * p + 2], 2.0, None,
            ALU.mult), reads=[ehrow], writes=[ehrow]))
        th.append(lambda p=p: b.op("tensor", lambda e: e.matmul(
            psQ[0:128, 514:516], ones_f[...],
            ehrow[0:1, 2 * p:2 * p + 2], start=True, stop=True,
            skip_group_check=True), reads=[ones_f, ehrow], writes=[psQ], acc=True))
        th.append(lambda p=p: b.op("vector", lambda e: e.tensor_copy(
            EX[:, 2 * p:2 * p + 2], psQ[0:128, 514:516]),
            reads=[psQ], writes=[EX]))
        return th

    psS.w = psA.w; psS.r = list(psA.r)
    psQ.w = psA.w; psQ.r = list(psA.r)

    for p in range(NCH):
        for t in qkv_pair_thunks(p):
            t()
    b.bigfree(*xnT)

    SBb = (psS, psQ)
    psBh = [Reg(Win(psB.h, k * 1024, 1024), f"psBh{k}") for k in range(2)]
    for v in psBh:
        v.w = psB.w
        v.r = list(psB.r)
    PexpV = [Reg(Win(Pexp.h, k * 1024, 1024), f"PexpV{k}") for k in range(4)]
    NPT = 8
    PTq = [b.big(f"PTq_{i}", 1024) for i in range(NPT)]
    RUv = [Reg(Win(RU.h.t, RU.h.c0 + k * 1024, 1024), f"RUv{k}")
           for k in range(2)]
    tile_i = [0]

    def emit_pv(h, qh, kb, pt):
        for qs in range(2):
            b.op("tensor", lambda e, h=h, qh=qh, kb=kb, qs=qs, pt=pt: e.matmul(
                psBh[qh][0:65, qs * 512:(qs + 1) * 512],
                vtok[kb][:, 65 * h:65 * h + 65],
                pt[0:128, qs * 512:(qs + 1) * 512],
                start=(kb == 0), stop=(kb == NTT - 1), skip_group_check=True),
                reads=[vtok[kb], pt], writes=[psBh[qh]], acc=True)

    def build_tail(h, qh):
        qc, r0 = h // 2, 64 * (h % 2)
        rB = 64 - r0
        selR = sel_hi if r0 == 0 else sel_lo
        selU = sel_lo if r0 == 0 else sel_hi
        pb, ruv = psBh[qh], RUv[qh]
        oc = slice(qh * 1024, (qh + 1) * 1024)

        def g_urow():
            b.op("vector", lambda e: e.tensor_scalar(
                urow[...], pb[64:65, 0:1024], float(T), None, ALU.add),
                reads=[pb], writes=[urow])

        def g_recip():
            def _f(e):
                with nc.allow_low_precision(reason="fp16 denom rows"):
                    return e.reciprocal(drec[...], urow[...])
            b.op("vector", _f, reads=[urow], writes=[drec])

        def g_rec2():
            return None

        def g_umul():
            def _f(e):
                with nc.allow_low_precision(reason="fp16 denom rows"):
                    return e.tensor_tensor(urow16[...], pb[64:65, 0:1024],
                                           drec[...], ALU.mult)
            b.op("vector", _f, reads=[pb, drec], writes=[urow16])

        def g_ru():
            X = SBb[(tile_i[0] + 1) % 2]
            for qs in range(2):
                b.op("tensor", lambda e, qs=qs, X=X: e.matmul(
                    X[0:128, qs * 512:(qs + 1) * 512], selR[...],
                    drec[0:1, qs * 512:(qs + 1) * 512], start=True,
                    stop=False), reads=[selR, drec], writes=[X], acc=True)
                b.op("tensor", lambda e, qs=qs, X=X: e.matmul(
                    X[0:128, qs * 512:(qs + 1) * 512], selU[...],
                    urow16[0:1, qs * 512:(qs + 1) * 512], start=False,
                    stop=True), reads=[selU, urow16], writes=[X], acc=True)
            b.op("scalar", lambda e, X=X: e.activation(
                ruv[...], X[0:128, 0:1024], ACT.Copy),
                reads=[X], writes=[ruv])

        def g_om():
            b.op("vector", lambda e: e.tensor_tensor(
                OT[qc][r0:r0 + 64, oc], pb[0:64, 0:1024],
                ruv[rB:rB + 64, 0:1024], ALU.mult),
                reads=[pb, ruv], writes=[OT[qc]])

        def g_os():
            b.op("vector", lambda e: e.scalar_tensor_tensor(
                OT[qc][r0:r0 + 64, oc], ruv[r0:r0 + 64, 0:1024],
                negvA[r0:r0 + 64, qc:qc + 1], OT[qc][r0:r0 + 64, oc],
                ALU.mult, ALU.add),
                reads=[ruv, negvA, OT[qc]], writes=[OT[qc]])
        return [g_urow, g_recip, g_rec2, g_umul, g_ru, g_om, g_os]

    tail_th = []
    for h in range(NH):
        qc, half = h // 2, h % 2
        r0 = 64 * half
        if half == 0:
            if qc > 0:
                b.bigfree(qT[qc - 1], kT[qc - 1])
            OT[qc] = b.big(f"OT_{qc}", T)
        for qh in range(2):
            pend = None
            for kb in range(NTT):
                i = tile_i[0]
                s = SBb[i % 2]
                for qs in range(2):
                    b.op("tensor", lambda e, s=s, qc=qc, r0=r0, kb=kb, qh=qh,
                         qs=qs: e.matmul(
                        s[0:128, qs * 512:(qs + 1) * 512],
                        kT[qc][r0:r0 + 64, kb * 128:(kb + 1) * 128],
                        qT[qc][r0:r0 + 64,
                               qh * 1024 + qs * 512:qh * 1024 + (qs + 1) * 512],
                        start=True, stop=True),
                        reads=[kT[qc], qT[qc]], writes=[s], acc=True)
                if pend is not None:
                    emit_pv(*pend)
                pend = (h, qh, kb, PTq[i % NPT])
                pe = PexpV[i % 4]
                b.op("scalar", lambda e, s=s, pe=pe, h=h: e.activation(
                    pe[...], s[0:128, 0:1024], ACT.Exp, scale=EX[:, h:h + 1]),
                    reads=[s, EX], writes=[pe])
                pt = PTq[i % NPT]
                b.op("vector", lambda e, pe=pe, pt=pt: e.tensor_scalar(
                    pt[...], pe[...], -1.0, None, ALU.add),
                    reads=[pe], writes=[pt])
                tile_i[0] += 1
                if tail_th:
                    tail_th.pop(0)()
            emit_pv(*pend)
            while tail_th:
                tail_th.pop(0)()
            tail_th = build_tail(h, qh)
    while tail_th:
        tail_th.pop(0)()
    b.barrier()
    p4.close()
    b.bigfree(qT[NCH - 1], kT[NCH - 1], *vtok, *PTq, RU)
    for c in range(NCH):
        b.dump(f"OT{c}", OT[c])

    xn1T = [b.big(f"xn1T_{c}", T) for c in range(NCH)]
    Wproj_s = [b.big(f"Wp_{c}", DM) for c in range(NCH)]
    p6 = stack.enter_context(contextlib.ExitStack())
    vpc = b.sb("vpc", [128, 8], F32, scoped=p6)
    srow0 = b.sb("srow0", [1, DM], F32, scoped=p6)
    stat6 = b.sb("stat6", [128, 8], F32, scoped=p6)
    p6w = stack.enter_context(contextlib.ExitStack())
    wpf = b.sb("wpf", [128, DM], F32, scoped=p6w)

    for c in range(NCH):
        b.op("tensor", lambda e, c=c: e.transpose(
            psA[:, 1024 + c:1024 + c + 1], vbp[0:1, c * 128:(c + 1) * 128],
            ident_f[0:1, 0:1]), reads=[vbp, ident_f], writes=[psA], acc=True)
    b.op("vector", lambda e: e.tensor_copy(vpc[...], psA[:, 1024:1032]),
         reads=[psA], writes=[vpc])
    for c in range(NCH):
        b.op("gpsimd", lambda e, c=c: e.dma_start(
            out=Wproj_s[c][...], in_=W_proj[c * 128:(c + 1) * 128, :]),
            writes=[Wproj_s[c]], dma=True)
        b.op("sync", lambda e, c=c: e.dma_start(
            out=wpf[...], in_=W_proj[c * 128:(c + 1) * 128, :]),
            writes=[wpf], dma=True)
        for fc in range(2):
            b.op("tensor", lambda e, c=c, fc=fc: e.matmul(
                psB[0:1, fc * 512:(fc + 1) * 512], vpc[:, c:c + 1],
                wpf[:, fc * 512:(fc + 1) * 512], start=(c == 0), stop=False,
                skip_group_check=True),
                reads=[vpc, wpf], writes=[psB], acc=True)
    b.op("vector", lambda e: e.tensor_tensor(
        srow0[...], psB[0:1, 0:1024], bp_row[...], ALU.add),
        reads=[psB, bp_row], writes=[srow0])
    b.barrier()
    p6w.close()
    g0r = [b.sb(f"g0r_{i}", [128, DM], F32, scoped=p6) for i in range(2)]
    x0r = [b.sb(f"x0r_{i}", [128, DM], F32, scoped=p6) for i in range(2)]
    x1t = [b.sb(f"x1t_{i}", [128, DM], F32, scoped=p6) for i in range(2)]
    tg6 = b.sb("tg6", [128, DM], F32, scoped=p6)
    xn1f = b.sb("xn1f", [128, DM], F32, scoped=p6)
    xsq6 = tg6

    def ln_stats6(xbuf):
        b.op("scalar", lambda e: e.activation(xsq6[...], xbuf[...], ACT.Square),
             reads=[xbuf], writes=[xsq6])
        b.op("vector", lambda e: e.reduce_sum(stat6[:, 0:1], xbuf[...], axis=AX.X),
             reads=[xbuf], writes=[stat6])
        b.op("vector", lambda e: e.reduce_sum(stat6[:, 1:2], xsq6[...], axis=AX.X),
             reads=[xsq6], writes=[stat6])
        b.op("vector", lambda e: e.tensor_scalar(
            stat6[:, 2:3], stat6[:, 0:1], 1.0 / DM, None, ALU.mult),
            reads=[stat6], writes=[stat6])
        b.op("vector", lambda e: e.tensor_tensor(
            stat6[:, 4:5], stat6[:, 2:3], stat6[:, 2:3], ALU.mult),
            reads=[stat6], writes=[stat6])
        b.op("vector", lambda e: e.tensor_scalar(
            stat6[:, 5:6], stat6[:, 1:2], 1.0 / DM, None, ALU.mult),
            reads=[stat6], writes=[stat6])
        b.op("vector", lambda e: e.tensor_tensor(
            stat6[:, 5:6], stat6[:, 5:6], stat6[:, 4:5], ALU.subtract),
            reads=[stat6], writes=[stat6])
        b.op("vector", lambda e: e.tensor_scalar(
            stat6[:, 5:6], stat6[:, 5:6], 1e-5, None, ALU.add),
            reads=[stat6], writes=[stat6])
        b.op("vector", lambda e: e.reciprocal(stat6[:, 6:7], stat6[:, 5:6]),
             reads=[stat6], writes=[stat6])
        b.op("scalar", lambda e: e.activation(
            stat6[:, 3:4], stat6[:, 6:7], ACT.Sqrt), reads=[stat6], writes=[stat6])
        b.op("vector", lambda e: e.tensor_tensor(
            stat6[:, 6:7], stat6[:, 3:4], stat6[:, 3:4], ALU.mult),
            reads=[stat6], writes=[stat6])
        b.op("vector", lambda e: e.tensor_tensor(
            stat6[:, 6:7], stat6[:, 6:7], stat6[:, 5:6], ALU.mult),
            reads=[stat6], writes=[stat6])
        b.op("vector", lambda e: e.tensor_scalar(
            stat6[:, 6:7], stat6[:, 6:7], -0.5, 1.5, ALU.mult, ALU.add),
            reads=[stat6], writes=[stat6])
        b.op("vector", lambda e: e.tensor_tensor(
            stat6[:, 3:4], stat6[:, 3:4], stat6[:, 6:7], ALU.mult),
            reads=[stat6], writes=[stat6])

    for tt in range(NTT):
        ps = psA if tt % 2 == 0 else psB
        gb, xb, x1b = g0r[tt % 2], x0r[tt % 2], x1t[tt % 2]
        b.op("gpsimd", lambda e, gb=gb, tt=tt: e.dma_start(
            out=gb[...], in_=g0_dram[...][tt * 128:(tt + 1) * 128, :]),
            reads=[g0_dram], writes=[gb], dma=True)
        b.op("gpsimd", lambda e, xb=xb, tt=tt: e.dma_start(
            out=xb[...], in_=x_in[tt * 128:(tt + 1) * 128, :]),
            writes=[xb], dma=True)
        for fc in range(2):
            for c in range(NCH):
                b.op("tensor", lambda e, c=c, fc=fc, tt=tt, pm=ps: e.matmul(
                    pm[:, fc * 512:(fc + 1) * 512],
                    OT[c][:, tt * 128:(tt + 1) * 128],
                    Wproj_s[c][:, fc * 512:(fc + 1) * 512],
                    start=(c == 0), stop=False),
                    reads=[OT[c], Wproj_s[c]], writes=[ps], acc=True)
            b.op("tensor", lambda e, fc=fc, pm=ps: e.matmul(
                pm[:, fc * 512:(fc + 1) * 512], ones_f[...],
                srow0[0:1, fc * 512:(fc + 1) * 512], start=False, stop=True),
                reads=[ones_f, srow0], writes=[ps], acc=True)
        b.op("vector", lambda e, gb=gb, pm=ps: e.tensor_tensor(
            tg6[...], pm[:, 0:1024], gb[...], ALU.mult),
            reads=[ps, gb], writes=[tg6])
        b.op("vector", lambda e, xb=xb, x1b=x1b: e.tensor_tensor(
            x1b[...], tg6[...], xb[...], ALU.add),
            reads=[tg6, xb], writes=[x1b])
        b.op("sync", lambda e, x1b=x1b, tt=tt: e.dma_start(
            out=x1_dram[...][tt * 128:(tt + 1) * 128, :], in_=x1b[...]),
            reads=[x1b], writes=[x1_dram], dma=True)
        ln_stats6(x1b)
        b.op("vector", lambda e, x1b=x1b: e.tensor_scalar(
            xn1f[...], x1b[...], stat6[:, 2:3], stat6[:, 3:4],
            ALU.subtract, ALU.mult), reads=[x1b, stat6], writes=[xn1f])
        for c in range(NCH):
            b.op("tensor", lambda e, c=c, pm=ps: e.transpose(
                pm[:, 1024 + c * 128:1024 + c * 128 + 128],
                xn1f[:, c * 128:(c + 1) * 128], ident_f[...]),
                reads=[xn1f, ident_f], writes=[ps], acc=True)
        for c in range(NCH):
            b.op("vector", lambda e, c=c, tt=tt, pm=ps: e.tensor_copy(
                xn1T[c][:, tt * 128:(tt + 1) * 128],
                pm[:, 1024 + c * 128:1024 + c * 128 + 128]),
                reads=[ps], writes=[xn1T[c]])
    b.barrier()
    p6.close()
    b.bigfree(*OT, *Wproj_s)
    for c in range(NCH):
        b.dump(f"xn1T{c}", xn1T[c])

    latentT = [b.big(f"latT_{i}", T) for i in range(2)]
    qrT = [b.big(f"qrT_{i}", T) for i in range(4)]
    krT = b.big("krT", T)
    Wq_s = [b.big(f"Wq_{c}", DM) for c in range(NCH)]
    Wku_s = [b.big(f"Wku_{i}", DM) for i in range(2)]
    Wvu_s = [b.big(f"Wvu_{i}", DM) for i in range(2)]
    cosPs = b.big("cosPs", T)
    sinPs = b.big("sinPs", T)
    p8 = stack.enter_context(contextlib.ExitStack())
    Wqr_s = [b.sb(f"Wqr_{c}", [128, 512], BF16, scoped=p8) for c in range(NCH)]
    Wkr_s = [b.sb(f"Wkr_{c}", [128, DR], BF16, scoped=p8) for c in range(NCH)]
    Wkvd_s = [b.sb(f"Wkvd_{c}", [128, DKV], BF16, scoped=p8) for c in range(NCH)]
    qrRaw = b.sb("qrRaw", [128, T], BF16, scoped=p8)
    qrotbf = b.sb("qrotbf", [128, T], BF16, scoped=p8)
    t81 = b.sb("t81", [128, T], F32, scoped=p8)
    t82 = b.sb("t82", [128, T], F32, scoped=p8)

    for (dst, hi) in ((cosPs, "cosP"), (sinPs, "sinP")):
        b.op("sync", sync_dma(t81[...], DN[hi]), writes=[t81], dma=True)
        b.op("vector", lambda e, dst=dst: e.tensor_copy(dst[...], t81[...]),
             reads=[t81], writes=[dst])
    for c in range(NCH):
        b.op("gpsimd", lambda e, c=c: e.dma_start(
            out=Wq_s[c][...], in_=Wq[c * 128:(c + 1) * 128, :]),
            writes=[Wq_s[c]], dma=True)
        b.op("gpsimd", lambda e, c=c: e.dma_start(
            out=Wqr_s[c][...], in_=Wqr[c * 128:(c + 1) * 128, :]),
            writes=[Wqr_s[c]], dma=True)
        b.op("gpsimd", lambda e, c=c: e.dma_start(
            out=Wkr_s[c][...], in_=Wkr[c * 128:(c + 1) * 128, :]),
            writes=[Wkr_s[c]], dma=True)
        b.op("gpsimd", lambda e, c=c: e.dma_start(
            out=Wkvd_s[c][...], in_=Wkvd[c * 128:(c + 1) * 128, :]),
            writes=[Wkvd_s[c]], dma=True)
    for i in range(2):
        b.op("gpsimd", lambda e, i=i: e.dma_start(
            out=Wku_s[i][...], in_=Wku[i * 128:(i + 1) * 128, :]),
            writes=[Wku_s[i]], dma=True)
        b.op("gpsimd", lambda e, i=i: e.dma_start(
            out=Wvu_s[i][...], in_=Wvu[i * 128:(i + 1) * 128, :]),
            writes=[Wvu_s[i]], dma=True)

    for lc in range(2):
        ps = psA if lc % 2 == 0 else psB
        for dc in range(NCH):
            for qs in range(4):
                b.op("tensor", lambda e, lc=lc, dc=dc, qs=qs, pm=ps: e.matmul(
                    pm[:, qs * 512:(qs + 1) * 512],
                    Wkvd_s[dc][:, lc * 128:(lc + 1) * 128],
                    xn1T[dc][:, qs * 512:(qs + 1) * 512],
                    start=(dc == 0), stop=(dc == NCH - 1)),
                    reads=[Wkvd_s[dc], xn1T[dc]], writes=[ps], acc=True)
        b.op("scalar", lambda e, lc=lc, pm=ps: e.activation(
            latentT[lc][...], pm[...], ACT.Copy),
            reads=[ps], writes=[latentT[lc]])

    def rope(dst, raw, rows, rot_ap, psy):
        for qs in range(4):
            b.op("tensor", lambda e, qs=qs: e.matmul(
                psy[0:rows, qs * 512:(qs + 1) * 512], rot_ap,
                raw[0:rows, qs * 512:(qs + 1) * 512], start=True, stop=True),
                reads=[raw, rot_s], writes=[psy], acc=True)
        b.op("scalar", lambda e: e.activation(
            qrotbf[0:rows, :], psy[0:rows, 0:2048], ACT.Copy),
            reads=[psy], writes=[qrotbf])
        b.op("vector", lambda e: e.tensor_tensor(
            t81[0:rows, :], raw[0:rows, :], cosPs[0:rows, :], ALU.mult),
            reads=[raw, cosPs], writes=[t81])
        b.op("vector", lambda e: e.tensor_tensor(
            t82[0:rows, :], qrotbf[0:rows, :], sinPs[0:rows, :], ALU.mult),
            reads=[qrotbf, sinPs], writes=[t82])
        b.op("vector", lambda e: e.tensor_tensor(
            dst[0:rows, :], t81[0:rows, :], t82[0:rows, :], ALU.add),
            reads=[t81, t82], writes=[dst])

    for qc in range(4):
        ps, psy = (psA, psB) if qc % 2 == 0 else (psB, psA)
        for dc in range(NCH):
            for qs in range(4):
                b.op("tensor", lambda e, qc=qc, dc=dc, qs=qs, pm=ps: e.matmul(
                    pm[:, qs * 512:(qs + 1) * 512],
                    Wqr_s[dc][:, qc * 128:(qc + 1) * 128],
                    xn1T[dc][:, qs * 512:(qs + 1) * 512],
                    start=(dc == 0), stop=(dc == NCH - 1)),
                    reads=[Wqr_s[dc], xn1T[dc]], writes=[ps], acc=True)
        b.op("scalar", lambda e, pm=ps: e.activation(
            qrRaw[...], pm[...], ACT.Copy), reads=[ps], writes=[qrRaw])
        rope(qrT[qc], qrRaw, 128, rot_s[...], psy)
    for dc in range(NCH):
        for qs in range(4):
            b.op("tensor", lambda e, dc=dc, qs=qs: e.matmul(
                psA[0:32, qs * 512:(qs + 1) * 512], Wkr_s[dc][...],
                xn1T[dc][:, qs * 512:(qs + 1) * 512],
                start=(dc == 0), stop=(dc == NCH - 1)),
                reads=[Wkr_s[dc], xn1T[dc]], writes=[psA], acc=True)
    b.op("scalar", lambda e: e.activation(
        qrRaw[0:32, :], psA[0:32, 0:2048], ACT.Copy),
        reads=[psA], writes=[qrRaw])
    rope(krT, qrRaw, 32, rot_s[0:32, 0:32], psB)
    b.barrier()
    p8.close()
    b.bigfree(cosPs, sinPs)
    b.dump("latT0", latentT[0])
    b.dump("qrT0", qrT[0])
    b.dump("krT", krT)

    OT1 = [None] * NCH
    qcat = [b.big(f"qcat_{i}", T) for i in range(4)]
    kcat = [b.big(f"kcat_{i}", T) for i in range(4)]
    v1h = [b.big(f"v1h_{i}", NTT * 65) for i in range(4)]
    PT1q = [b.big(f"PT1q_{i}", 1024) for i in range(4)]
    p9 = stack.enter_context(contextlib.ExitStack())
    mcols = b.sb("mcols", [128, NTT + 4], F32, scoped=p9)
    mcolR = [Reg(Win(mcols.h, c, 1), f"mc{c}") for c in range(NTT)]
    tmpR = [Reg(Win(mcols.h, NTT + c, 1), f"mt{c}") for c in range(4)]
    cem2 = [b.sb(f"cem_{i}", [1, T], F32, scoped=p9) for i in range(2)]
    Dsb = b.sb("Dsb", [1, T], F32, scoped=p9)
    drec1 = b.sb("drec1", [1, T], BF16, scoped=p9)
    lnr1 = b.sb("lnr1", [1, 1024], F32, scoped=p9)
    Rsb1 = b.sb("Rsb1", [64, T], BF16, scoped=p9)
    psAh = psA.h
    S1b = [Reg(Win(psAh, k * 512, 512), f"S1q{k}") for k in range(4)]
    for v in S1b:
        v.w = psA.w
        v.r = list(psA.r)

    for i in range(4):
        b.op("gpsimd", lambda e, i=i: e.memset(
            v1h[i][...].rearrange("p (t j) -> p t j", j=65)[:, :, 64:65], 1.0),
            writes=[v1h[i]])
    for i in (0, 2):
        b.op("gpsimd", lambda e, i=i: e.memset(kcat[i][96:97, :], -1.0),
             writes=[kcat[i]])
        b.op("sync", lambda e, i=i: e.dma_start(
            out=kcat[i][64:96, :], in_=krT[0:32, :]),
            reads=[krT], writes=[kcat[i]], dma=True)
    b.bigfree(krT)

    def build_units(h):
        qlo, qhi = qcat[h % 4], qcat[(h + 1) % 4]
        klo, khi = kcat[h % 4], kcat[(h + 1) % 4]
        th = []

        def g_q(qs, h=h):
            i = tile1_i[0]
            tile1_i[0] += 1
            X = S1b[i % 4]
            for dc in range(NCH):
                b.op("tensor", lambda e, dc=dc, qs=qs, h=h, X=X: e.matmul(
                    X[0:128, 0:512],
                    Wq_s[dc][:, 64 * h:64 * h + 128],
                    xn1T[dc][:, qs * 512:(qs + 1) * 512],
                    start=(dc == 0), stop=(dc == NCH - 1)),
                    reads=[Wq_s[dc], xn1T[dc]], writes=[X], acc=True)
            b.op("scalar", lambda e, qs=qs, X=X: e.activation(
                qlo[0:64, qs * 512:qs * 512 + 512],
                X[0:64, 0:512], ACT.Copy), reads=[X], writes=[qlo])
            b.op("scalar", lambda e, qs=qs, X=X: e.activation(
                qhi[64:128, qs * 512:qs * 512 + 512],
                X[64:128, 0:512], ACT.Copy), reads=[X], writes=[qhi])
        th += [lambda qs=qs: g_q(qs) for qs in range(4)]
        th.append(lambda: b.op("gpsimd", lambda e: e.dma_start(
            out=qhi[0:64, :], in_=qhi[64:128, :]),
            reads=[qhi], writes=[qhi], dma=True))
        for dh in range(2):
            qb_ = qlo if dh == 0 else qhi
            th.append(lambda h=h + dh, qb_=qb_: b.op(
                "sync", lambda e: e.dma_start(
                    out=qb_[64:96, :],
                    in_=qrT[h // 4][32 * (h % 4):32 * (h % 4) + 32, :]),
                reads=[qrT[h // 4]], writes=[qb_], dma=True))

        def g_k(qs, h=h):
            i = tile1_i[0]
            tile1_i[0] += 1
            X = S1b[i % 4]
            for lc in range(2):
                b.op("tensor", lambda e, lc=lc, qs=qs, h=h, X=X: e.matmul(
                    X[0:128, 0:512],
                    Wku_s[lc][:, 64 * h:64 * h + 128],
                    latentT[lc][:, qs * 512:(qs + 1) * 512],
                    start=(lc == 0), stop=(lc == 1)),
                    reads=[Wku_s[lc], latentT[lc]], writes=[X], acc=True)
            b.op("scalar", lambda e, qs=qs, X=X: e.activation(
                klo[0:64, qs * 512:qs * 512 + 512],
                X[0:64, 0:512], ACT.Copy), reads=[X], writes=[klo])
            b.op("scalar", lambda e, qs=qs, X=X: e.activation(
                khi[64:128, qs * 512:qs * 512 + 512],
                X[64:128, 0:512], ACT.Copy), reads=[X], writes=[khi])
        th += [lambda qs=qs: g_k(qs) for qs in range(4)]
        th.append(lambda: b.op("gpsimd", lambda e: e.dma_start(
            out=khi[0:64, :], in_=khi[64:128, :]),
            reads=[khi], writes=[khi], dma=True))
        th.append(lambda: b.op("gpsimd", lambda e: e.dma_start(
            out=khi[64:96, :], in_=klo[64:96, :]),
            reads=[klo], writes=[khi], dma=True))
        th.append(lambda: b.op("gpsimd", lambda e: e.memset(
            khi[96:97, :], -1.0), writes=[khi]))

        def g_v(dh, vg, h=h):
            vb_ = v1h[(h + dh) % 4]
            i = tile1_i[0]
            tile1_i[0] += 1
            X = S1b[i % 4]
            for tt in range(8 * vg, 8 * vg + 8):
                for lc in range(2):
                    b.op("tensor", lambda e, tt=tt, lc=lc, hh=h + dh, X=X:
                         e.matmul(
                        X[:, (tt % 8) * 64:(tt % 8) * 64 + 64],
                        latentT[lc][:, tt * 128:(tt + 1) * 128],
                        Wvu_s[lc][:, 64 * hh:64 * hh + 64],
                        start=(lc == 0), stop=(lc == 1)),
                        reads=[latentT[lc], Wvu_s[lc]], writes=[X], acc=True)
            b.op("scalar", lambda e, vg=vg, vb_=vb_, X=X: e.activation(
                vb_[...].rearrange("p (t j) -> p t j", j=65)
                [:, 8 * vg:8 * vg + 8, 0:64],
                X[:, 0:512].rearrange("p (t j) -> p t j", j=64), ACT.Copy),
                reads=[X], writes=[vb_])
        for dh in range(2):
            for vg in range(2):
                th.append(lambda dh=dh, vg=vg: g_v(dh, vg))
        return th

    tile1_i = [0]
    pump_q = []

    def emit_pv1(h, kb, a, bnd, pt):
        vb_ = v1h[h % 4]
        c = a
        while c < bnd:
            ce = min((c // 512 + 1) * 512, bnd)
            b.op("tensor", lambda e, kb=kb, c=c, ce=ce, a=a, pt=pt,
                 vb_=vb_: e.matmul(
                psB[0:65, c:ce], vb_[:, 65 * kb:65 * kb + 65],
                pt[0:128, c - a:ce - a], start=(kb == 0),
                stop=(kb == 4 * (c // 512) + 3), skip_group_check=True),
                reads=[vb_, pt], writes=[psB], acc=True)
            c = ce

    def do_p1(h, qb, a, bnd):
        qb_, kb_ = qcat[h % 4], kcat[h % 4]
        i = tile1_i[0]
        tile1_i[0] += 1
        X = S1b[i % 4]
        b.op("tensor", lambda e, qb=qb, a=a, bnd=bnd, X=X,
             qb_=qb_, kb_=kb_: e.matmul(
            X[0:128, 0:bnd - a],
            qb_[0:96, qb * 128:(qb + 1) * 128],
            kb_[0:96, a:bnd], start=True, stop=True),
            reads=[qb_, kb_], writes=[X], acc=True)
        d0 = qb * 128
        if a <= d0 < bnd:
            b.op("vector", lambda e, d0=d0, a=a, X=X: e.tensor_tensor(
                X[0:128, d0 - a:d0 - a + 128],
                X[0:128, d0 - a:d0 - a + 128], caus_add_s[...],
                ALU.add), reads=[X, caus_add_s], writes=[X])
        if a == 0:
            b.op("vector", lambda e, qb=qb, bnd=bnd, X=X: e.reduce_max(
                mcols[:, qb:qb + 1], X[0:128, 0:bnd], axis=AX.X),
                reads=[X], writes=[mcolR[qb]])
        else:
            tm = tmpR[i % 4]
            b.op("vector", lambda e, bnd=bnd, a=a, X=X, tm=tm: e.reduce_max(
                tm[0:128, 0:1], X[0:128, 0:bnd - a], axis=AX.X),
                reads=[X], writes=[tm])
            b.op("vector", lambda e, qb=qb, tm=tm: e.tensor_tensor(
                mcols[:, qb:qb + 1], mcols[:, qb:qb + 1],
                tm[0:128, 0:1], ALU.max),
                reads=[tm, mcolR[qb]], writes=[mcolR[qb]])

    def p1_units(h):
        for qb in range(NTT):
            kw = (qb + 1) * 128
            a = 0
            while a < kw:
                bnd = min(a + 512, kw)
                yield (qb, a, bnd)
                a = bnd

    def mrow(h):
        qb_ = qcat[h % 4]
        cemX = cem2[h % 2]
        for qb in range(NTT):
            Xm = S1b[qb // 4]
            b.op("tensor", lambda e, qb=qb: e.transpose(
                psAh[0:1, qb * 128:(qb + 1) * 128], mcols[:, qb:qb + 1],
                ident_f[...]), reads=[mcolR[qb], ident_f], writes=[Xm],
                acc=True)
        b.op("scalar", lambda e, qb_=qb_: e.activation(
            qb_[96:97, :], psAh[0:1, 0:2048], ACT.Copy),
            reads=S1b, writes=[qb_])
        b.op("vector", lambda e, qb_=qb_: e.tensor_tensor(
            Dsb[...], psAh[0:1, 0:2048], qb_[96:97, :], ALU.subtract),
            reads=S1b + [qb_], writes=[Dsb])
        b.op("scalar", lambda e, h=h, cemX=cemX: e.activation(
            cemX[...], Dsb[...], ACT.Exp, scale=L1SCALE,
            bias=lobo_row[0:1, h:h + 1]),
            reads=[Dsb, lobo_row], writes=[cemX])

    def tail_half(h, k):
        hc, hr = h // 2, 64 * (h % 2)
        cemX = cem2[h % 2]
        ks = slice(k * 512, (k + 1) * 512)
        b.op("vector", lambda e: e.tensor_tensor(
            Dsb[0:1, ks], psB[64:65, ks], cemX[0:1, ks], ALU.add),
            reads=[psB, cemX], writes=[Dsb])
        b.op("scalar", lambda e: e.activation(
            lnr1[0:1, 0:512], Dsb[0:1, ks], ACT.Ln), reads=[Dsb],
            writes=[lnr1])
        b.op("scalar", lambda e: e.activation(
            drec1[0:1, ks], lnr1[0:1, 0:512], ACT.Exp, scale=-1.0),
            reads=[lnr1], writes=[drec1])
        X = S1b[(tile1_i[0] + 1) % 4]
        b.op("tensor", lambda e, X=X: e.matmul(
            X[0:64, 0:512], sel_lo[0:1, 0:64], drec1[0:1, ks],
            start=True, stop=True),
            reads=[sel_lo, drec1], writes=[X], acc=True)
        b.op("scalar", lambda e, X=X: e.activation(
            Rsb1[0:64, ks], X[0:64, 0:512], ACT.Copy),
            reads=[X], writes=[Rsb1])
        b.op("vector", lambda e: e.tensor_tensor(
            OT1[hc][hr:hr + 64, ks], psB[0:64, ks], Rsb1[0:64, ks],
            ALU.mult), reads=[psB, Rsb1], writes=[OT1[hc]])

    def merged_step(h):
        qb_, kb_ = qcat[h % 4], kcat[h % 4]
        g1 = p1_units(h + 1) if h + 1 < NH else iter(())
        pend = None
        for qtile in range(4):
            if h > 0:
                tail_half(h - 1, qtile)
            for kb in range(min(4 * qtile + 4, NTT)):
                a = max(kb * 128, qtile * 512)
                bnd = (qtile + 1) * 512
                i = tile1_i[0]
                tile1_i[0] += 1
                X = S1b[i % 4]
                pt = PT1q[i % 4]
                b.op("tensor", lambda e, kb=kb, a=a, bnd=bnd, qtile=qtile,
                     X=X, qb_=qb_, kb_=kb_: e.matmul(
                    X[0:128, a - qtile * 512:bnd - qtile * 512],
                    kb_[0:97, kb * 128:(kb + 1) * 128],
                    qb_[0:97, a:bnd], start=True, stop=True),
                    reads=[kb_, qb_], writes=[X], acc=True)
                if pend is not None:
                    emit_pv1(*pend)
                pend = (h, kb, a, bnd, pt)
                b.op("scalar", lambda e, a=a, qtile=qtile, bnd=bnd, X=X,
                     pt=pt: e.activation(
                    pt[0:128, 0:bnd - a],
                    X[0:128, a - qtile * 512:bnd - qtile * 512],
                    ACT.Exp, scale=L1SCALE), reads=[X], writes=[pt])
                if a == kb * 128:
                    b.op("vector", lambda e, pt=pt: e.tensor_tensor(
                        pt[0:128, 0:128], pt[0:128, 0:128], caus_mul_s[...],
                        ALU.mult), reads=[pt, caus_mul_s], writes=[pt])
                t1 = next(g1, None)
                if t1 is not None:
                    do_p1(h + 1, *t1)
                if pump_q:
                    pump_q.pop(0)()
        emit_pv1(*pend)
        for t1 in g1:
            do_p1(h + 1, *t1)
            if pump_q:
                pump_q.pop(0)()
        while pump_q:
            pump_q.pop(0)()
        if h + 1 < NH:
            mrow(h + 1)

    for t in build_units(0):
        t()
    pump_q = build_units(2)
    for u in p1_units(0):
        do_p1(0, *u)
        if pump_q:
            pump_q.pop(0)()
    mrow(0)
    for h in range(NH):
        if h in (1, 5, 9):
            b.bigfree(qrT[(h - 1) // 4])
        if h == 13:
            b.bigfree(qrT[3], *latentT, *Wku_s, *Wvu_s)
        if h % 2 == 0:
            OT1[h // 2] = b.big(f"OT1_{h // 2}", T)
        merged_step(h)
        if h % 2 == 1 and h + 3 < NH:
            pump_q += build_units(h + 3)
    for k in range(4):
        tail_half(NH - 1, k)
    b.barrier()
    p9.close()
    b.bigfree(*Wq_s, *qcat, *kcat, *v1h, *PT1q)
    for c in range(NCH):
        b.dump(f"OT1_{c}", OT1[c])

    xkT = [b.big(f"xkT_{c}", T) for c in range(NCH)]
    xrT = [b.big(f"xrT_{c}", T) for c in range(NCH)]
    p10m = stack.enter_context(contextlib.ExitStack())
    td8 = b.sb("td8", [128, T], BF16, scoped=p10m)
    for c in range(NCH):
        for (dstl, mc) in ((xkT, 8 + c), (xrT, c)):
            dst = dstl[c]
            b.op("vector", lambda e, c=c, dst=dst, mc=mc: e.tensor_scalar(
                dst[0:128, 0:1], xn1T[c][:, 0:1], mu_col[:, mc:mc + 1], None,
                ALU.mult), reads=[xn1T[c], mu_col], writes=[dst])
            b.op("vector", lambda e, c=c: e.tensor_tensor(
                td8[:, 0:2047], xn1T[c][:, 1:2048], xn1T[c][:, 0:2047],
                ALU.subtract), reads=[xn1T[c]], writes=[td8])
            b.op("vector", lambda e, c=c, dst=dst, mc=mc: e.scalar_tensor_tensor(
                dst[0:128, 1:2048], td8[:, 0:2047], mu_col[:, mc:mc + 1],
                xn1T[c][:, 0:2047], ALU.mult, ALU.add),
                reads=[td8, mu_col, xn1T[c]], writes=[dst])
    b.barrier()
    p10m.close()
    b.bigfree(*xn1T)

    kactT = [b.big(f"kactT_{c}", T) for c in range(NCH)]
    p10k = stack.enter_context(contextlib.ExitStack())
    WkT_s = [b.sb(f"WkT_{c}", [128, DM], BF16, scoped=p10k) for c in range(NCH)]
    ktmp = b.sb("ktmp", [128, T], F32, scoped=p10k)
    for cc in range(NCH):
        b.op("gpsimd", lambda e, cc=cc: e.dma_start(
            out=WkT_s[cc][...],
            in_=wT_dram[...][1][cc * 128:(cc + 1) * 128, :]),
            reads=[wT_dram], writes=[WkT_s[cc]], dma=True)
    for do in range(NCH):
        ps = psA if do % 2 == 0 else psB
        for cc in range(NCH):
            for qs in range(4):
                b.op("tensor", lambda e, do=do, cc=cc, qs=qs, pm=ps: e.matmul(
                    pm[:, qs * 512:(qs + 1) * 512],
                    WkT_s[cc][:, do * 128:(do + 1) * 128],
                    xkT[cc][:, qs * 512:(qs + 1) * 512],
                    start=(cc == 0), stop=(cc == NCH - 1)),
                    reads=[WkT_s[cc], xkT[cc]], writes=[ps], acc=True)
        b.op("scalar", lambda e, pm=ps: e.activation(
            ktmp[...], pm[...], ACT.Relu), reads=[ps], writes=[ktmp])
        b.op("scalar", lambda e, do=do: e.activation(
            kactT[do][...], ktmp[...], ACT.Square),
            reads=[ktmp], writes=[kactT[do]])
    b.barrier()
    p10k.close()
    b.bigfree(*xkT)

    WrT_s = [b.big(f"WrT_{c}", DM) for c in range(NCH)]
    WvT_s = [b.big(f"WvT_{c}", DM) for c in range(NCH)]
    Wo_s = [b.big(f"Wo_{c}", DM) for c in range(NCH)]
    p10 = stack.enter_context(contextlib.ExitStack())
    rtt = b.sb("rtt", [128, DM], F32, scoped=p10)
    tkv = b.sb("tkv", [128, DM], F32, scoped=p10)
    g1t = b.sb("g1t", [128, DM], F32, scoped=p10)
    tg1 = b.sb("tg1", [128, DM], F32, scoped=p10)
    x1r = [b.sb(f"x1r_{i}", [128, DM], F32, scoped=p10) for i in range(2)]
    x2t = [b.sb(f"x2t_{i}", [128, DM], F32, scoped=p10) for i in range(2)]
    for cc in range(NCH):
        b.op("gpsimd", lambda e, cc=cc: e.dma_start(
            out=WrT_s[cc][...],
            in_=wT_dram[...][0][cc * 128:(cc + 1) * 128, :]),
            reads=[wT_dram], writes=[WrT_s[cc]], dma=True)
        b.op("gpsimd", lambda e, cc=cc: e.dma_start(
            out=WvT_s[cc][...],
            in_=wT_dram[...][2][cc * 128:(cc + 1) * 128, :]),
            reads=[wT_dram], writes=[WvT_s[cc]], dma=True)
        b.op("gpsimd", lambda e, cc=cc: e.dma_start(
            out=Wo_s[cc][...], in_=Wo[cc * 128:(cc + 1) * 128, :]),
            writes=[Wo_s[cc]], dma=True)
    for tt in range(NTT):
        xrb = x1r[tt % 2]
        x2b = x2t[tt % 2]
        b.op("gpsimd", lambda e, xrb=xrb, tt=tt: e.dma_start(
            out=xrb[...], in_=x1_dram[...][tt * 128:(tt + 1) * 128, :]),
            reads=[x1_dram], writes=[xrb], dma=True)
        for fc in range(2):
            for cc in range(NCH):
                b.op("tensor", lambda e, fc=fc, cc=cc, tt=tt: e.matmul(
                    psA[:, fc * 512:(fc + 1) * 512],
                    xrT[cc][:, tt * 128:(tt + 1) * 128],
                    WrT_s[cc][:, fc * 512:(fc + 1) * 512],
                    start=(cc == 0), stop=(cc == NCH - 1)),
                    reads=[xrT[cc], WrT_s[cc]], writes=[psA], acc=True)
                b.op("tensor", lambda e, fc=fc, cc=cc, tt=tt: e.matmul(
                    psA[:, 1024 + fc * 512:1024 + (fc + 1) * 512],
                    kactT[cc][:, tt * 128:(tt + 1) * 128],
                    WvT_s[cc][:, fc * 512:(fc + 1) * 512],
                    start=(cc == 0), stop=(cc == NCH - 1)),
                    reads=[kactT[cc], WvT_s[cc]], writes=[psA], acc=True)
                b.op("tensor", lambda e, fc=fc, cc=cc, tt=tt: e.matmul(
                    psB[:, fc * 512:(fc + 1) * 512],
                    OT1[cc][:, tt * 128:(tt + 1) * 128],
                    Wo_s[cc][:, fc * 512:(fc + 1) * 512],
                    start=(cc == 0), stop=(cc == NCH - 1)),
                    reads=[OT1[cc], Wo_s[cc]], writes=[psB], acc=True)
        b.op("scalar", lambda e: e.activation(
            rtt[...], psA[:, 0:1024], ACT.Sigmoid), reads=[psA], writes=[rtt])
        b.op("vector", lambda e: e.tensor_tensor(
            tkv[...], psA[:, 1024:2048], rtt[...], ALU.mult),
            reads=[psA, rtt], writes=[tkv])
        b.op("scalar", lambda e: e.activation(
            g1t[...], tkv[...], ACT.Silu), reads=[tkv], writes=[g1t])
        b.op("vector", lambda e: e.tensor_tensor(
            tg1[...], psB[:, 0:1024], g1t[...], ALU.mult),
            reads=[psB, g1t], writes=[tg1])
        b.op("vector", lambda e, xrb=xrb, x2b=x2b: e.tensor_tensor(
            x2b[...], tg1[...], xrb[...], ALU.add),
            reads=[tg1, xrb], writes=[x2b])
        b.op("sync", lambda e, x2b=x2b, tt=tt: e.dma_start(
            out=y_out[tt * 128:(tt + 1) * 128, :], in_=x2b[...]),
            reads=[x2b], writes=[y_reg], dma=True)
    for i in range(NLANE):
        key = f"sync_d{i}"
        b._wait("sync", key, b.esem[key][1])
    b.barrier()

    return nc, b, DN, y_reg, (g0_dram, x1_dram, wT_dram), dict(locals())



_IN_MAP_NAMES = {
    "W_qkv": "L0_attn_W_qkv", "b_qkv": "L0_attn_b_qkv",
    "W_proj": "L0_attn_W_proj", "b_proj": "L0_attn_b_proj",
    "fbias": "L0_ffn_bias",
    "Wq": "L1_attn_Wq", "Wqr": "L1_attn_Wqr", "Wkr": "L1_attn_Wkr",
    "Wkvd": "L1_attn_Wkvd", "Wku": "L1_attn_Wku", "Wvu": "L1_attn_Wvu",
    "Wo": "L1_attn_Wo", "lobo": "L1_attn_lobo_log",
    "Wr": "L1_ffn_W_r", "Wk": "L1_ffn_W_k", "Wv": "L1_ffn_W_v",
    "mu_r": "L1_ffn_mu_r", "mu_k": "L1_ffn_mu_k",
}


def make_in_maps(inputs):
    consts = host_consts()
    x = np.ascontiguousarray(np.asarray(inputs["L0_attn_x"], dtype=np.float32))
    base = {k: np.ascontiguousarray(np.asarray(inputs[v], np.float32))
            for k, v in _IN_MAP_NAMES.items()}
    base["tern"] = np.ascontiguousarray(
        np.asarray(inputs["L0_ffn_ternary_weight"], np.int32))
    base["ascale"] = np.asarray(
        inputs["L0_ffn_act_scale"], np.float32).reshape([1])
    base.update(consts)
    return [dict(base, x=x[core]) for core in range(8)]


def kernel(**inputs):
    with contextlib.ExitStack() as stack:
        nc, b, DN, y_reg, scratch, env = build_nc(stack)
        b.finalize()
        in_maps = make_in_maps(inputs)
        res = run_bass_kernel_spmd(nc, in_maps, core_ids=list(range(8)))
    return np.stack([r["y"] for r in res.results], axis=0)


# version 32
# speedup vs baseline: 1.0353x; 1.0353x over previous
import contextlib
import math
import numpy as np
import concourse.bass as bass
import concourse.mybir as mybir
from concourse.bass_utils import run_bass_kernel_spmd

F32 = mybir.dt.float32
BF16 = mybir.dt.float16
I32 = mybir.dt.int32
ACT = mybir.ActivationFunctionType
ALU = mybir.AluOpType
AX = mybir.AxisListType

T = 2048
DM = 1024
NTT = T // 128
NCH = DM // 128
NH = 16
HD = 64
DR = 32
DKV = 256
L1SCALE = 1.0 / math.sqrt(96.0)
MAGIC32 = 1.5 * 2.0 ** 23
MAGIC64 = 1.5 * 2.0 ** 52

ENGS = ("tensor", "scalar", "vector", "gpsimd", "sync")
DEBUG = []


def host_consts():
    ident = np.eye(128, dtype=np.float32)
    tri = np.tril(np.ones((128, 128), dtype=bool))
    caus_add = np.where(tri, 0.0, -1e30).astype(np.float32)
    caus_mul = np.where(tri.T, 1.0, 0.0).astype(np.float32)
    rot = np.zeros((128, 128), dtype=np.float32)
    for blk in range(4):
        b0 = 32 * blk
        for d in range(16):
            rot[b0 + d + 16, b0 + d] = -1.0
            rot[b0 + d, b0 + d + 16] = 1.0
    half = np.float32(16.0)
    inv = (np.float32(1.0)
           / np.float32(10000.0) ** (np.arange(16, dtype=np.float32) / half))
    ang = (np.arange(T, dtype=np.float32)[:, None]
           * inv[None, :].astype(np.float32)).astype(np.float32)
    cos16 = np.cos(ang).astype(np.float32).T
    sin16 = np.sin(ang).astype(np.float32).T
    cosP = np.tile(np.concatenate([cos16, cos16], 0), (4, 1))
    sinP = np.tile(np.concatenate([sin16, sin16], 0), (4, 1))
    return {"ident": ident, "caus_add": caus_add, "caus_mul": caus_mul,
            "rot": rot, "cosP": cosP, "sinP": sinP}


class Reg:
    __slots__ = ("h", "w", "r", "name")

    def __init__(self, h, name):
        self.h = h
        self.name = name
        self.w = None
        self.r = []

    def __getitem__(self, idx):
        return self.h[idx]


class Win:
    __slots__ = ("t", "c0", "w")

    def __init__(self, t, c0, w):
        self.t, self.c0, self.w = t, c0, w

    def __getitem__(self, idx):
        if idx is Ellipsis:
            return self.t[:, self.c0:self.c0 + self.w]
        rows, cols = idx
        if isinstance(cols, slice):
            a = self.c0 + (cols.start or 0)
            bnd = self.c0 + (cols.stop if cols.stop is not None else self.w)
            return self.t[rows, a:bnd]
        raise TypeError(cols)


NLANE = 8
SEMKEYS = (("tensor", "scalar", "vector", "gpsimd")
           + tuple(f"sync_d{i}" for i in range(NLANE))
           + tuple(f"gpsimd_d{i}" for i in range(NLANE)))
ENG_OF = {k: ("sync" if k.startswith("sync_d") else
              "gpsimd" if k.startswith("gpsimd_d") else k) for k in SEMKEYS}


class B:
    def __init__(self, nc, stack):
        self.nc = nc
        self.stack = stack
        self.streams = {e: [] for e in ENGS}
        self.esem = {}
        for k in SEMKEYS:
            h = stack.enter_context(nc.semaphore("sem_" + k))
            self.esem[k] = [h, 0]
        self.hiwater = {e: {} for e in ENGS}
        self.drained = {e: 0 for e in ENGS}
        self.emitted = {e: 0 for e in ENGS}
        self.dma_idx = {"sync": 0, "gpsimd": 0}
        self.dbg_out = {}

    def sb(self, name, shape, dtype, scoped=None):
        st = scoped if scoped is not None else self.stack
        h = st.enter_context(self.nc.sbuf_tensor(name, shape, dtype))
        return Reg(h, name)

    def ps(self, name, shape, dtype=F32, scoped=None):
        st = scoped if scoped is not None else self.stack
        h = st.enter_context(self.nc.psum_tensor(name, shape, dtype))
        return Reg(h, name)

    def dreg(self, ap, name):
        return Reg(ap, name)

    BIGW = 74752

    def big(self, name, ncols):
        if not hasattr(self, "bigbuf"):
            self.bigbuf = self.stack.enter_context(
                self.nc.sbuf_tensor("BIG", [128, self.BIGW], BF16))
            self.big_free = [(0, self.BIGW)]
            self.big_deps = []
        ncols = (ncols + 15) // 16 * 16
        for i, (a, w) in enumerate(self.big_free):
            if w >= ncols:
                self.big_free[i] = (a + ncols, w - ncols)
                reg = Reg(Win(self.bigbuf, a, ncols), name)
                keep = []
                for (lo, hi, evs) in self.big_deps:
                    if lo < a + ncols and hi > a:
                        reg.r.extend(evs)
                    else:
                        keep.append((lo, hi, evs))
                self.big_deps = keep
                return reg
        raise MemoryError(f"big arena full allocating {name} ({ncols} cols); "
                          f"free={self.big_free}")

    def bigfree(self, *regs):
        for reg in regs:
            win = reg.h
            self.big_free.append((win.c0, win.w))
            evs = list(reg.r)
            if reg.w is not None:
                evs.append(reg.w)
            if evs:
                self.big_deps.append((win.c0, win.c0 + win.w, evs))
        self.big_free.sort()
        out = []
        for a, w in self.big_free:
            if out and out[-1][0] + out[-1][1] == a:
                out[-1] = (out[-1][0], out[-1][1] + w)
            else:
                out.append((a, w))
        self.big_free = [tuple(x) for x in out]

    def _wait(self, eng, srckey, val):
        if srckey == eng and eng in ("tensor", "scalar", "vector", "gpsimd"):
            if val > self.drained[eng]:
                self.streams[eng].append(lambda E: E.drain())
                self.drained[eng] = self.esem[eng][1]
        else:
            hw = self.hiwater[eng].get(srckey, 0)
            if val > hw:
                h = self.esem[srckey][0]
                self.streams[eng].append(lambda E, h=h, v=val: E.wait_ge(h, v))
                self.hiwater[eng][srckey] = val

    def _edge(self, eng, ev):
        if ev is not None:
            self._wait(eng, ev[0], ev[1])

    def op(self, eng, fn, reads=(), writes=(), acc=False, dma=False):
        if dma:
            assert eng in ("sync", "gpsimd")
            idx = self.dma_idx[eng]
            self.dma_idx[eng] = idx + 1
            key = f"{eng}_d{idx % NLANE}"
            if idx >= NLANE:
                self._wait(eng, key, 16 * (idx // NLANE))
        else:
            assert eng != "sync"
            key = eng
        for rg in reads:
            self._edge(eng, rg.w)
        for rg in writes:
            if not (acc and rg.w is not None and rg.w[0] == key):
                self._edge(eng, rg.w)
            for ev in rg.r:
                self._edge(eng, ev)
        sem = self.esem[key]
        k = 16 if dma else 1
        sem[1] += k
        n = sem[1]
        h = sem[0]

        def run(E, fn=fn, h=h, k=k):
            fn(E).then_inc(h, k)
        self.streams[eng].append(run)
        self.emitted[eng] += 1
        ev = (key, n)
        for rg in writes:
            rg.w = ev
            rg.r = []
        for rg in reads:
            rg.r.append(ev)
        return ev

    def fence(self, eng, ev):
        self._edge(eng, ev)

    def barrier(self):
        evs = []
        for e in ("tensor", "scalar", "vector", "gpsimd"):
            sem = self.esem[e]
            sem[1] += 1
            h, n = sem[0], sem[1]
            self.streams[e].append(lambda E, h=h: E.drain().then_inc(h, 1))
            self.emitted[e] += 1
            evs.append((e, n))
        for i in range(NLANE):
            evs.append((f"sync_d{i}", self.esem[f"sync_d{i}"][1]))
            evs.append((f"gpsimd_d{i}", self.esem[f"gpsimd_d{i}"][1]))
        for e in ENGS:
            for ev in evs:
                if ev[0] != e:
                    self._wait(e, ev[0], ev[1])
            if e in self.esem:
                self.drained[e] = self.esem[e][1]
        return evs

    def finalize(self):
        with self.nc.Block() as block:
            for eng in ENGS:
                dec = getattr(block, eng)
                lst = self.streams[eng]

                def make(lst_):
                    def run(E):
                        for fn in lst_:
                            fn(E)
                    return run
                dec(make(lst))

    def dump(self, name, reg, ap=None):
        if name not in DEBUG:
            return
        a = reg[...] if ap is None else ap
        shp = list(a.shape)
        d = self.nc.dram_tensor("dbg_" + name, shp, a.dtype,
                                kind="ExternalOutput").ap()
        self.op("sync", lambda e, d=d, a=a: e.dma_start(out=d, in_=a),
                reads=[reg], dma=True)
        self.dbg_out[name] = shp


def build_nc(stack):
    nc = bass.Bass("TRN2", target_bir_lowering=False, debug=False)
    b = B(nc, stack)

    DN = {}

    def din(name, shape, dtype=F32):
        DN[name] = nc.dram_tensor(name, shape, dtype, kind="ExternalInput").ap()
        return DN[name]

    x_in = din("x", [T, DM])
    W_qkv = din("W_qkv", [DM, 3 * DM])
    b_qkv = din("b_qkv", [3 * DM])
    W_proj = din("W_proj", [DM, DM])
    b_proj = din("b_proj", [DM])
    tern = din("tern", [DM, DM], I32)
    fbias = din("fbias", [DM])
    ascale = din("ascale", [1])
    Wq = din("Wq", [DM, DM])
    Wqr = din("Wqr", [DM, 512])
    Wkr = din("Wkr", [DM, DR])
    Wkvd = din("Wkvd", [DM, DKV])
    Wku = din("Wku", [DKV, DM])
    Wvu = din("Wvu", [DKV, DM])
    Wo = din("Wo", [DM, DM])
    lobo = din("lobo", [NH])
    Wr = din("Wr", [DM, DM])
    Wk = din("Wk", [DM, DM])
    Wv = din("Wv", [DM, DM])
    mu_r = din("mu_r", [DM])
    mu_k = din("mu_k", [DM])
    for nm in ("ident", "caus_add", "caus_mul", "rot"):
        din(nm, [128, 128])
    din("cosP", [128, T])
    din("sinP", [128, T])

    y_out = nc.dram_tensor("y", [T, DM], F32, kind="ExternalOutput").ap()
    g0_dram = b.dreg(nc.dram_tensor("g0_s", [T, DM], F32, kind="Internal").ap(), "g0d")
    x1_dram = b.dreg(nc.dram_tensor("x1_s", [T, DM], F32, kind="Internal").ap(), "x1d")
    wT_dram = b.dreg(nc.dram_tensor("wT_s", [3, DM, DM], BF16, kind="Internal").ap(), "wTd")
    y_reg = b.dreg(y_out, "y")

    ident_f = b.sb("ident_f", [128, 128], F32)
    caus_add_s = b.sb("caus_add_s", [128, 128], F32)
    caus_mul_s = b.sb("caus_mul_s", [128, 128], BF16)
    rot_s = b.sb("rot_s", [128, 128], BF16)
    ones_bf = b.sb("ones_bf", [1, 128], BF16)
    ones_f = b.sb("ones_f", [1, 128], F32)
    sel64 = b.sb("sel64", [128, 2], BF16)
    sel_lo = b.sb("sel_lo", [1, 128], BF16)
    onescol_f = b.sb("onescol_f", [128, 1], F32)
    xnbrow = b.sb("xnbrow", [1, DM], F32)
    sel_hi = b.sb("sel_hi", [1, 128], BF16)
    bq_col = b.sb("bq_col", [128, 24], F32)
    bp_row = b.sb("bp_row", [1, DM], F32)
    fbs_row = b.sb("fbs_row", [1, DM], F32)
    bv_row = b.sb("bv_row", [1, DM], F32)
    lobo_row = b.sb("lobo_row", [1, NH], F32)
    mu_col = b.sb("mu_col", [128, 16], F32)
    smal = b.sb("smal", [128, 16], F32)
    C_SCALE, C_INVS, C_ALPHA, C_E, C_075, C_AS, C_SC1, C_IS1 = range(8)

    psA = b.ps("psA", [128, 2048], F32)
    psB = b.ps("psB", [128, 2048], F32)

    sync_dma = lambda dst, src: (lambda e, d=dst, s=src: e.dma_start(out=d, in_=s))
    gp_dma = sync_dma

    for dst, srcn in ((ident_f, "ident"), (caus_add_s, "caus_add")):
        b.op("sync", sync_dma(dst[...], DN[srcn]), writes=[dst], dma=True)
    for dst, srcn in ((caus_mul_s, "caus_mul"), (rot_s, "rot")):
        b.op("gpsimd", gp_dma(dst[...], DN[srcn]), writes=[dst], dma=True)
    p0 = stack.enter_context(contextlib.ExitStack())
    rows_tmp = b.sb("rows_tmp", [1, 3072 + 2048], F32, scoped=p0)
    fb_row = b.sb("fb_row", [1, DM], F32, scoped=p0)
    b.op("sync", sync_dma(rows_tmp[0:1, 0:3072],
                          b_qkv.rearrange("(a n) -> a n", a=1)),
         writes=[rows_tmp], dma=True)
    b.op("sync", sync_dma(rows_tmp[0:1, 3072:4096],
                          mu_r.rearrange("(a n) -> a n", a=1)),
         writes=[rows_tmp], dma=True)
    b.op("sync", sync_dma(rows_tmp[0:1, 4096:5120],
                          mu_k.rearrange("(a n) -> a n", a=1)),
         writes=[rows_tmp], dma=True)
    b.op("sync", sync_dma(fb_row[...], fbias.rearrange("(a n) -> a n", a=1)),
         writes=[fb_row], dma=True)
    b.op("sync", sync_dma(lobo_row[...], lobo.rearrange("(a n) -> a n", a=1)),
         writes=[lobo_row], dma=True)
    b.op("sync", sync_dma(smal[0:1, C_AS:C_AS + 1], ascale.rearrange("(a n) -> a n", a=1)),
         writes=[smal], dma=True)
    for c in range(24):
        b.op("tensor", lambda e, c=c: e.transpose(
            psB[:, c:c + 1], rows_tmp[0:1, c * 128:(c + 1) * 128],
            ident_f[0:1, 0:1]),
            reads=[rows_tmp, ident_f], writes=[psB], acc=True)
    for c in range(16):
        b.op("tensor", lambda e, c=c: e.transpose(
            psB[:, 24 + c:24 + c + 1],
            rows_tmp[0:1, 3072 + c * 128:3072 + (c + 1) * 128],
            ident_f[0:1, 0:1]),
            reads=[rows_tmp, ident_f], writes=[psB], acc=True)
    b.op("vector", lambda e: e.tensor_copy(bq_col[...], psB[:, 0:24]),
         reads=[psB], writes=[bq_col])
    b.op("vector", lambda e: e.tensor_copy(mu_col[...], psB[:, 24:40]),
         reads=[psB], writes=[mu_col])
    b.op("sync", sync_dma(bp_row[...], b_proj.rearrange("(a n) -> a n", a=1)),
         writes=[bp_row], dma=True)
    b.op("sync", sync_dma(bv_row[...], b_qkv.rearrange("(a n) -> a n", a=1)[:, 2048:3072]),
         writes=[bv_row], dma=True)
    b.op("gpsimd", lambda e: e.memset(ones_bf[...], 1.0), writes=[ones_bf])
    b.op("gpsimd", lambda e: e.memset(ones_f[...], 1.0), writes=[ones_f])
    b.op("gpsimd", lambda e: e.memset(sel64[...], 0.0), writes=[sel64])
    b.op("gpsimd", lambda e: e.memset(sel_lo[...], 0.0), writes=[sel_lo])
    b.op("gpsimd", lambda e: e.memset(sel_lo[0:1, 0:64], 1.0), writes=[sel_lo])
    b.op("gpsimd", lambda e: e.memset(sel_hi[...], 0.0), writes=[sel_hi])
    b.op("gpsimd", lambda e: e.memset(onescol_f[...], 1.0), writes=[onescol_f])
    b.op("gpsimd", lambda e: e.memset(sel_hi[0:1, 64:128], 1.0), writes=[sel_hi])
    b.op("gpsimd", lambda e: e.memset(sel64[0:64, 0:1], 1.0), writes=[sel64])
    b.op("gpsimd", lambda e: e.memset(sel64[64:128, 1:2], 1.0), writes=[sel64])
    b.op("gpsimd", lambda e: e.memset(smal[:, C_075:C_075 + 1], 0.75),
         writes=[smal])
    for col, mg in ((8, MAGIC32), (9, -MAGIC32), (10, MAGIC64), (11, -MAGIC64)):
        b.op("gpsimd", lambda e, col=col, mg=mg: e.memset(
            smal[:, col:col + 1], mg), writes=[smal])
        nc.const_aps.register(F32, mg, smal.h[:, col:col + 1])
    b.op("vector", lambda e: e.tensor_scalar(
        smal[:, C_E:C_E + 1], smal[:, C_075:C_075 + 1], MAGIC32, -MAGIC32,
        ALU.add, ALU.add), reads=[smal], writes=[smal])
    b.op("vector", lambda e: e.tensor_scalar(
        smal[:, C_ALPHA:C_ALPHA + 1], smal[:, C_E:C_E + 1], -0.75, 4.0,
        ALU.add, ALU.mult), reads=[smal], writes=[smal])
    b.op("vector", lambda e: e.tensor_scalar(
        smal[0:1, C_SC1:C_SC1 + 1], smal[0:1, C_AS:C_AS + 1], 1e-5, None,
        ALU.max), reads=[smal], writes=[smal])
    b.op("vector", lambda e: e.reciprocal(
        smal[0:1, C_IS1:C_IS1 + 1], smal[0:1, C_SC1:C_SC1 + 1]),
        reads=[smal], writes=[smal])
    b.op("vector", lambda e: e.tensor_scalar(
        fbs_row[...], fb_row[...], smal[0:1, C_IS1:C_IS1 + 1], None, ALU.mult),
        reads=[fb_row, smal], writes=[fbs_row])
    b.op("tensor", lambda e: e.matmul(
        psA[:, 0:2], ones_f[...], smal[0:1, C_SC1:C_IS1 + 1], start=True,
        stop=True), reads=[ones_f, smal], writes=[psA])
    b.op("vector", lambda e: e.tensor_copy(smal[:, C_SCALE:C_INVS + 1],
                                           psA[:, 0:2]),
         reads=[psA], writes=[smal])
    b.barrier()
    p0.close()

    w0T = [b.big(f"w0T_{c}", DM) for c in range(NCH)]
    p1 = stack.enter_context(contextlib.ExitStack())
    tmpi = [b.sb(f"p1i_{i}", [128, DM], I32, scoped=p1) for i in range(2)]
    tmpf = [b.sb(f"p1f_{i}", [128, DM], F32, scoped=p1) for i in range(2)]
    wtt2 = [b.sb(f"p1wt{i}", [128, DM], BF16, scoped=p1) for i in range(2)]
    wttV2 = [[Reg(Win(w.h, cc * 128, 128), f"wtt{i}_{cc}") for cc in range(NCH)]
             for i, w in enumerate(wtt2)]

    for oc in range(NCH):
        ti, tf = tmpi[oc % 2], tmpf[oc % 2]
        b.op("gpsimd", gp_dma(ti[...], tern[oc * 128:(oc + 1) * 128, :]),
             writes=[ti], dma=True)
        b.op("vector", lambda e, ti=ti, tf=tf: e.tensor_scalar(
            tf[...], ti[...], -1.0, None, ALU.add), reads=[ti], writes=[tf])
        ps = psA if oc % 2 == 0 else psB
        for ic in range(NCH):
            b.op("tensor", lambda e, tf=tf, ic=ic, pm=ps: e.transpose(
                pm[:, ic * 128:ic * 128 + 128],
                tf[:, ic * 128:(ic + 1) * 128], ident_f[...]),
                reads=[tf, ident_f], writes=[ps], acc=True)
        for ic in range(NCH):
            b.op("vector", lambda e, ic=ic, oc=oc, pm=ps: e.tensor_copy(
                w0T[ic][:, oc * 128:(oc + 1) * 128],
                pm[:, ic * 128:ic * 128 + 128]),
                reads=[ps], writes=[w0T[ic]])
    for wi, wsrc in enumerate((Wr, Wk, Wv)):
        for dc in range(NCH):
            tf = tmpf[dc % 2]
            b.op("gpsimd", gp_dma(tf[...], wsrc[dc * 128:(dc + 1) * 128, :]),
                 writes=[tf], dma=True)
            ps = psA if dc % 2 == 0 else psB
            for cc in range(NCH):
                b.op("tensor", lambda e, tf=tf, cc=cc, pm=ps: e.transpose(
                    pm[:, cc * 128:cc * 128 + 128],
                    tf[:, cc * 128:(cc + 1) * 128], ident_f[...]),
                    reads=[tf, ident_f], writes=[ps], acc=True)
            wtt = wtt2[dc % 2]
            wttV = wttV2[dc % 2]
            for cc in range(NCH):
                b.op("scalar", lambda e, cc=cc, pm=ps, wtt=wtt: e.activation(
                    wtt[:, cc * 128:(cc + 1) * 128],
                    pm[:, cc * 128:cc * 128 + 128], ACT.Copy),
                    reads=[ps], writes=[wttV[cc]])
            b.op("sync", lambda e, wi=wi, dc=dc, wtt=wtt: e.dma_start(
                out=wT_dram[...][wi].rearrange("(a p) n -> p a n", p=128)
                [:, :, dc * 128:(dc + 1) * 128],
                in_=wtt[...].rearrange("p (a c) -> p a c", c=128)),
                reads=wttV, writes=[wT_dram], dma=True)
    b.barrier()
    p1.close()

    xnT = [b.big(f"xnT_{c}", T) for c in range(NCH)]
    xiT = [b.big(f"xiT_{c}", T) for c in range(NCH)]
    p2 = stack.enter_context(contextlib.ExitStack())
    x0t = [b.sb(f"x0t_{i}", [128, DM], F32, scoped=p2) for i in range(2)]
    vq = [b.sb(f"vq_{i}", [128, DM], F32, scoped=p2) for i in range(2)]
    sgn2 = [b.sb(f"sgn_{i}", [128, DM], F32, scoped=p2) for i in range(2)]
    rq22 = [b.sb(f"rq2_{i}", [128, DM], F32, scoped=p2) for i in range(2)]
    xn_f = [b.sb("xn_f_0", [128, DM], F32, scoped=p2)] * 2
    statw = b.sb("statw", [128, 112], F32, scoped=p2)
    sumR = [Reg(Win(statw.h, t, 1), f"su{t}") for t in range(NTT)]
    sqR = [Reg(Win(statw.h, NTT + t, 1), f"sq{t}") for t in range(NTT)]
    C_SU, C_SQ, C_M, C_V, C_R, C_T, C_7 = 0, 16, 32, 48, 64, 80, 96
    stw = Reg(Win(statw.h, 32, 80), "stw")

    for tt in range(NTT):
        xb = x0t[tt % 2]
        sq = vq[tt % 2]
        b.op("sync", sync_dma(xb[...], x_in[tt * 128:(tt + 1) * 128, :]),
             writes=[xb], dma=True)
        b.op("scalar", lambda e, xb=xb, sq=sq: e.activation(
            sq[...], xb[...], ACT.Square), reads=[xb], writes=[sq])
        b.op("vector", lambda e, xb=xb, tt=tt: e.reduce_sum(
            statw[:, tt:tt + 1], xb[...], axis=AX.X),
            reads=[xb], writes=[sumR[tt]])
        b.op("vector", lambda e, tt=tt, sq=sq: e.reduce_sum(
            statw[:, NTT + tt:NTT + tt + 1], sq[...], axis=AX.X),
            reads=[sq], writes=[sqR[tt]])

    def sc(dst, src, s1, s2, o1, o2=None):
        def _f(e):
            if o2 is None:
                return e.tensor_scalar(statw[:, dst:dst + 16],
                                       statw[:, src:src + 16], s1, s2, o1)
            return e.tensor_scalar(statw[:, dst:dst + 16],
                                   statw[:, src:src + 16], s1, s2, o1, o2)
        b.op("vector", _f, reads=[stw] + sumR + sqR, writes=[stw])

    def tc(dst, a, c2, o):
        b.op("vector", lambda e: e.tensor_tensor(
            statw[:, dst:dst + 16], statw[:, a:a + 16], statw[:, c2:c2 + 16],
            o), reads=[stw], writes=[stw])
    sc(C_M, C_SU, 1.0 / DM, None, ALU.mult)
    tc(C_T, C_M, C_M, ALU.mult)
    sc(C_V, C_SQ, 1.0 / DM, None, ALU.mult)
    tc(C_V, C_V, C_T, ALU.subtract)
    sc(C_V, C_V, 1e-5, None, ALU.add)
    b.op("vector", lambda e: e.reciprocal(
        statw[:, C_T:C_T + 16], statw[:, C_V:C_V + 16]),
        reads=[stw], writes=[stw])
    b.op("scalar", lambda e: e.activation(
        statw[:, C_R:C_R + 16], statw[:, C_T:C_T + 16], ACT.Sqrt),
        reads=[stw], writes=[stw])
    tc(C_T, C_R, C_R, ALU.mult)
    tc(C_T, C_T, C_V, ALU.mult)
    sc(C_T, C_T, -0.5, 1.5, ALU.mult, ALU.add)
    tc(C_R, C_R, C_T, ALU.mult)
    b.op("vector", lambda e: e.tensor_scalar(
        statw[:, C_7:C_7 + 16], statw[:, C_R:C_R + 16],
        smal[:, C_INVS:C_INVS + 1], None, ALU.mult),
        reads=[stw, smal], writes=[stw])

    pA0 = Reg(Win(psA.h, 0, 1024), "pA0")
    pB0 = Reg(Win(psB.h, 0, 1024), "pB0")
    pB1 = Reg(Win(psB.h, 1024, 1024), "pB1")
    for v, src in ((pA0, psA), (pB0, psB), (pB1, psB)):
        v.w = src.w
        v.r = list(src.r)
    for tt in range(NTT):
        xb = x0t[tt % 2]
        xn = xn_f[tt % 2]
        vb = vq[tt % 2]
        sgn = sgn2[tt % 2]
        rq2 = rq22[tt % 2]
        b.op("sync", sync_dma(xb[...], x_in[tt * 128:(tt + 1) * 128, :]),
             writes=[xb], dma=True)
        b.op("vector", lambda e, xb=xb, xn=xn, tt=tt: e.tensor_scalar(
            xn[...], xb[...], statw[:, C_M + tt:C_M + tt + 1],
            statw[:, C_R + tt:C_R + tt + 1],
            ALU.subtract, ALU.mult), reads=[xb, stw], writes=[xn])
        b.op("vector", lambda e, xb=xb, vb=vb, tt=tt: e.tensor_scalar(
            vb[...], xb[...], statw[:, C_M + tt:C_M + tt + 1],
            statw[:, C_7 + tt:C_7 + tt + 1],
            ALU.subtract, ALU.mult), reads=[xb, stw], writes=[vb])
        b.op("vector", lambda e, vb=vb, sgn=sgn: e.tensor_scalar(
            sgn[...], vb[...], MAGIC32, -MAGIC32, ALU.add, ALU.add),
            reads=[vb], writes=[sgn])
        b.op("scalar", lambda e, vb=vb, rq2=rq2: e.add(
            rq2[...], vb[...], MAGIC64), reads=[vb, smal], writes=[rq2])
        b.op("scalar", lambda e, rq2=rq2: e.add(
            rq2[...], rq2[...], -MAGIC64), reads=[smal], writes=[rq2])
        b.op("vector", lambda e, sgn=sgn, rq2=rq2: e.tensor_tensor(
            sgn[...], sgn[...], rq2[...], ALU.subtract),
            reads=[sgn, rq2], writes=[sgn])
        b.op("vector", lambda e, sgn=sgn, rq2=rq2: e.scalar_tensor_tensor(
            sgn[...], sgn[...], smal[:, C_ALPHA:C_ALPHA + 1], rq2[...],
            ALU.mult, ALU.add), reads=[sgn, smal, rq2], writes=[sgn])
        b.op("vector", lambda e, vb=vb, sgn=sgn: e.tensor_scalar(
            vb[...], sgn[...], 127.0, -127.0, ALU.min, ALU.max),
            reads=[sgn], writes=[vb])
        for fc in range(2):
            b.op("tensor", lambda e, xn=xn, fc=fc, tt=tt: e.matmul(
                psB[0:1, 1024 + fc * 512:1024 + (fc + 1) * 512], onescol_f[...],
                xn[:, fc * 512:(fc + 1) * 512], start=(tt == 0), stop=False,
                skip_group_check=True),
                reads=[onescol_f, xn], writes=[pB1], acc=True)
        for c in range(NCH):
            b.op("tensor", lambda e, c=c, xn=xn: e.transpose(
                psA[:, c * 128:c * 128 + 128], xn[:, c * 128:(c + 1) * 128],
                ident_f[...]), reads=[xn, ident_f], writes=[pA0], acc=True)
            b.op("tensor", lambda e, c=c, vb=vb: e.transpose(
                psB[:, c * 128:c * 128 + 128], vb[:, c * 128:(c + 1) * 128],
                ident_f[...]), reads=[vb, ident_f], writes=[pB0], acc=True)
        for c in range(NCH):
            b.op("vector", lambda e, c=c, tt=tt: e.tensor_copy(
                xnT[c][:, tt * 128:(tt + 1) * 128],
                psA[:, c * 128:c * 128 + 128]), reads=[pA0], writes=[xnT[c]])
            b.op("scalar", lambda e, c=c, tt=tt: e.activation(
                xiT[c][:, tt * 128:(tt + 1) * 128],
                psB[:, c * 128:c * 128 + 128], ACT.Copy),
                reads=[pB0], writes=[xiT[c]])
    b.op("vector", lambda e: e.tensor_scalar(
        xnbrow[...], psB[0:1, 1024:2048], 1.0 / T, None, ALU.mult),
        reads=[pB1], writes=[xnbrow])
    b.barrier()
    p2.close()
    for c in range(NCH):
        b.dump(f"xnT{c}", xnT[c])
        b.dump(f"xiT{c}", xiT[c])

    p3 = stack.enter_context(contextlib.ExitStack())
    g0t = [b.sb(f"g0t_{i}", [128, DM], F32, scoped=p3) for i in range(2)]
    for tt in range(NTT):
        ps = psA if tt % 2 == 0 else psB
        gb = g0t[tt % 2]
        for fc in range(2):
            for c in range(NCH):
                b.op("tensor", lambda e, c=c, fc=fc, tt=tt, pm=ps: e.matmul(
                    pm[:, fc * 512:(fc + 1) * 512],
                    xiT[c][:, tt * 128:(tt + 1) * 128],
                    w0T[c][:, fc * 512:(fc + 1) * 512],
                    start=(c == 0), stop=False),
                    reads=[xiT[c], w0T[c]], writes=[ps], acc=True)
            b.op("tensor", lambda e, fc=fc, pm=ps: e.matmul(
                pm[:, fc * 512:(fc + 1) * 512], ones_f[...],
                fbs_row[0:1, fc * 512:(fc + 1) * 512], start=False, stop=True),
                reads=[ones_f, fbs_row], writes=[ps], acc=True)
        b.op("scalar", lambda e, gb=gb, pm=ps: e.activation(
            gb[...], pm[:, 0:1024], ACT.Silu, scale=smal[:, C_SCALE:C_SCALE + 1]),
            reads=[ps, smal], writes=[gb])
        b.op("sync", lambda e, gb=gb, tt=tt: e.dma_start(
            out=g0_dram[...][tt * 128:(tt + 1) * 128, :], in_=gb[...]),
            reads=[gb], writes=[g0_dram], dma=True)
    b.barrier()
    p3.close()
    b.bigfree(*xiT, *w0T)
    b.dump("g0", g0_dram)

    qT = [b.big(f"qT_{c}", T) for c in range(NCH)]
    kT = [b.big(f"kT_{c}", T) for c in range(NCH)]
    vtok = [b.big(f"vtok_{t}", NH * 65) for t in range(NTT)]
    EX = b.sb("EX", [128, NH], F32)
    vbp = b.sb("vbp", [1, DM], F32)
    vAcol = b.sb("vAcol", [128, 8], F32)
    negvA = b.sb("negvA", [128, 8], F32)
    p4 = stack.enter_context(contextlib.ExitStack())
    wqsl = [b.sb(f"wqsl_{i}", [128, DM], BF16, scoped=p4) for i in range(2)]
    xnb = b.sb("xnb", [128, 8], F32, scoped=p4)
    qsq = b.sb("qsq", [128, T], BF16, scoped=p4)
    fro82 = b.sb("fro82", [2, NCH], F32, scoped=p4)
    infc = b.sb("infc", [128, NCH + 2], F32, scoped=p4)
    ehrow = b.sb("ehrow", [1, NH], F32, scoped=p4)
    ones16 = b.sb("ones16", [128, 1], BF16, scoped=p4)
    urow = b.sb("urow", [1, 1024], F32, scoped=p4)
    RU = b.big("RU", T)
    OT = [None] * NCH
    psS = Reg(Win(psA.h, 0, 1024), "psS")
    psQ = Reg(Win(psA.h, 1024, 1024), "psQ")

    p4v = contextlib.ExitStack()
    wvsl = [b.sb(f"wvsl_{i}", [128, 4096], BF16, scoped=p4v) for i in range(2)]
    wvf = b.sb("wvf", [128, 1024], F32, scoped=p4v)
    b.op("gpsimd", lambda e: e.memset(ones16[...], 1.0), writes=[ones16])
    for t in range(NTT):
        b.op("gpsimd", lambda e, t=t: e.memset(
            vtok[t][...].rearrange("p (h j) -> p h j", j=65)[:, :, 64:65], 1.0),
            writes=[vtok[t]])
    for c in range(NCH):
        b.op("tensor", lambda e, c=c: e.transpose(
            psQ[0:128, 512 + c:512 + c + 1], xnbrow[0:1, c * 128:(c + 1) * 128],
            ident_f[0:1, 0:1]), reads=[xnbrow, ident_f], writes=[psQ], acc=True)
    b.op("vector", lambda e: e.tensor_copy(xnb[...], psQ[0:128, 512:520]),
         reads=[psQ], writes=[xnb])
    for fc in range(2):
        wb = wvsl[fc]
        b.op("gpsimd", lambda e, wb=wb, fc=fc: e.dma_start(
            out=wb[...].rearrange("p (a n) -> p a n", n=512),
            in_=W_qkv.rearrange("(a p) n -> p a n", p=128)
            [:, :, 2048 + fc * 512:2048 + (fc + 1) * 512]),
            writes=[wb], dma=True)
        for hf in range(4):
            b.op("sync", lambda e, fc=fc, hf=hf: e.dma_start(
                out=wvf[...].rearrange("p (a n) -> p a n", n=512),
                in_=W_qkv.rearrange("(a p) n -> p a n", p=128)
                [:, 2 * hf:2 * hf + 2, 2048 + fc * 512:2048 + (fc + 1) * 512]),
                writes=[wvf], dma=True)
            for a in range(2):
                b.op("tensor", lambda e, fc=fc, hf=hf, a=a: e.matmul(
                    psB[32:33, 1024 + fc * 512:1024 + (fc + 1) * 512],
                    xnb[:, 2 * hf + a:2 * hf + a + 1],
                    wvf[:, a * 512:(a + 1) * 512],
                    start=(hf == 0 and a == 0), stop=False,
                    skip_group_check=True),
                    reads=[xnb, wvf], writes=[psB], acc=True)
    for tt in range(NTT):
        ps = psA if tt % 2 == 0 else psB
        for fc in range(2):
            for dc in range(NCH):
                b.op("tensor", lambda e, fc=fc, dc=dc, tt=tt, pm=ps: e.matmul(
                    pm[:, fc * 512:(fc + 1) * 512],
                    xnT[dc][:, tt * 128:(tt + 1) * 128],
                    wvsl[fc][:, dc * 512:(dc + 1) * 512],
                    start=(dc == 0), stop=False),
                    reads=[xnT[dc], wvsl[fc]], writes=[ps], acc=True)
            b.op("tensor", lambda e, fc=fc, pm=ps: e.matmul(
                pm[:, fc * 512:(fc + 1) * 512], ones_f[...],
                bv_row[0:1, fc * 512:(fc + 1) * 512], start=False, stop=True),
                reads=[ones_f, bv_row], writes=[ps], acc=True)
        b.op("scalar", lambda e, tt=tt, pm=ps: e.activation(
            vtok[tt][...].rearrange("p (h j) -> p h j", j=65)[:, :, 0:64],
            pm[:, 0:1024].rearrange("p (h j) -> p h j", j=64), ACT.Copy),
            reads=[ps], writes=[vtok[tt]])
    b.op("vector", lambda e: e.tensor_tensor(
        vbp[...], psB[32:33, 1024:2048], bv_row[...], ALU.add),
        reads=[psB, bv_row], writes=[vbp])
    for tt in range(NTT):
        for (a0, a1) in ((0, 512), (512, 1024), (1024, 1040)):
            b.op("tensor", lambda e, tt=tt, a0=a0, a1=a1: e.matmul(
                psB[0:1, a0:a1], ones16[...], vtok[tt][:, a0:a1],
                start=(tt == 0), stop=(tt == NTT - 1), skip_group_check=True),
                reads=[ones16, vtok[tt]], writes=[psB], acc=True)
    b.op("vector", lambda e: e.tensor_scalar(
        qsq[0:1, 0:1040], psB[0:1, 0:1040], 1.0 / T, None, ALU.mult),
        reads=[psB], writes=[qsq])
    for c in range(NCH):
        for half in range(2):
            h2 = 2 * c + half
            b.op("vector", lambda e, half=half, h2=h2: e.tensor_copy(
                urow[0:1, 64 * half:64 * half + 64],
                qsq[0:1, 65 * h2:65 * h2 + 64]),
                reads=[qsq], writes=[urow])
        b.op("tensor", lambda e, c=c: e.transpose(
            psQ[0:128, 520 + c:520 + c + 1], urow[0:1, 0:128], ident_f[0:1, 0:1]),
            reads=[urow, ident_f], writes=[psQ], acc=True)
    b.op("vector", lambda e: e.tensor_copy(vAcol[...], psQ[0:128, 520:528]),
         reads=[psQ], writes=[vAcol])
    b.op("vector", lambda e: e.tensor_scalar(
        negvA[...], vAcol[...], -1.0, None, ALU.mult),
        reads=[vAcol], writes=[negvA])
    for c in range(NCH):
        b.op("scalar", lambda e, c=c: e.activation(
            qsq[...], xnT[c][...], ACT.Abs), reads=[xnT[c]], writes=[qsq])
        b.op("vector", lambda e, c=c: e.reduce_sum(
            infc[:, c:c + 1], qsq[...], axis=AX.X), reads=[qsq], writes=[infc])
    b.op("vector", lambda e: e.reduce_max(
        infc[:, 8:9], infc[:, 0:8], axis=AX.X), reads=[infc], writes=[infc])
    b.op("tensor", lambda e: e.transpose(
        psQ[0:1, 528:656], infc[:, 8:9], ident_f[...]),
        reads=[infc, ident_f], writes=[psQ])
    b.op("vector", lambda e: e.reduce_max(
        infc[0:1, 9:10], psQ[0:1, 528:656], axis=AX.X),
        reads=[psQ], writes=[infc])
    p4v.close()
    Pexp = b.sb("Pexp", [128, 4096], F32, scoped=p4)
    drec = b.sb("drec", [1, 1024], BF16, scoped=p4)
    urow16 = b.sb("urow16", [1, 1024], BF16, scoped=p4)
    lnrow = b.sb("lnrow", [1, 1024], F32, scoped=p4)

    def qkv_pair_thunks(p):
        th = []
        for which in range(2):
            dst = qT[p] if which == 0 else kT[p]
            col0 = p * 128 if which == 0 else 1024 + p * 128
            bcol = p if which == 0 else 8 + p
            wb = wqsl[(2 * p + which) % 2]
            th.append(lambda wb=wb, col0=col0: b.op(
                "gpsimd", lambda e: e.dma_start(
                    out=wb[...].rearrange("p (a n) -> p a n", n=128),
                    in_=W_qkv.rearrange("(a p) n -> p a n", p=128)
                    [:, :, col0:col0 + 128]), writes=[wb], dma=True))
            for half in range(4):
                lo = (half % 2) * 512
                for dc in range(NCH):
                    th.append(lambda wb=wb, dc=dc, half=half, lo=lo: b.op(
                        "tensor", lambda e: e.matmul(
                            psQ[0:128, lo:lo + 512],
                            wb[:, dc * 128:(dc + 1) * 128],
                            xnT[dc][:, half * 512:(half + 1) * 512],
                            start=(dc == 0), stop=(dc == NCH - 1)),
                        reads=[wb, xnT[dc]], writes=[psQ], acc=True))
                th.append(lambda dst=dst, half=half, bcol=bcol, lo=lo: b.op(
                    "vector", lambda e: e.tensor_scalar(
                        dst[0:128, half * 512:(half + 1) * 512],
                        psQ[0:128, lo:lo + 512],
                        bq_col[:, bcol:bcol + 1], None, ALU.add),
                    reads=[psQ, bq_col], writes=[dst]))
        th.append(lambda p=p: b.op("scalar", lambda e: e.activation(
            qsq[...], qT[p][...], ACT.Square), reads=[qT[p]], writes=[qsq]))
        for qs in range(4):
            th.append(lambda qs=qs: b.op("tensor", lambda e: e.matmul(
                psQ[0:2, 0:512], sel64[...], qsq[:, qs * 512:(qs + 1) * 512],
                start=(qs == 0), stop=(qs == 3), skip_group_check=True),
                reads=[sel64, qsq], writes=[psQ], acc=True))
        th.append(lambda p=p: b.op("vector", lambda e: e.reduce_sum(
            fro82[0:2, p:p + 1], psQ[0:2, 0:512], axis=AX.X),
            reads=[psQ], writes=[fro82]))
        th.append(lambda p=p: b.op("tensor", lambda e: e.transpose(
            psQ[0:1, 512:514], fro82[0:2, p:p + 1],
            ident_f[0:2, 0:2]), reads=[fro82, ident_f], writes=[psQ], acc=True))
        th.append(lambda p=p: b.op("scalar", lambda e: e.activation(
            ehrow[0:1, 2 * p:2 * p + 2], psQ[0:1, 512:514],
            ACT.Sqrt), reads=[psQ], writes=[ehrow]))
        th.append(lambda p=p: b.op("vector", lambda e: e.tensor_scalar(
            ehrow[0:1, 2 * p:2 * p + 2], ehrow[0:1, 2 * p:2 * p + 2],
            infc[0:1, 9:10], None, ALU.mult), reads=[ehrow, infc], writes=[ehrow]))
        th.append(lambda p=p: b.op("vector", lambda e: e.reciprocal(
            ehrow[0:1, 2 * p:2 * p + 2], ehrow[0:1, 2 * p:2 * p + 2]),
            reads=[ehrow], writes=[ehrow]))
        th.append(lambda p=p: b.op("vector", lambda e: e.tensor_scalar(
            ehrow[0:1, 2 * p:2 * p + 2], ehrow[0:1, 2 * p:2 * p + 2], 2.0, None,
            ALU.mult), reads=[ehrow], writes=[ehrow]))
        th.append(lambda p=p: b.op("tensor", lambda e: e.matmul(
            psQ[0:128, 514:516], ones_f[...],
            ehrow[0:1, 2 * p:2 * p + 2], start=True, stop=True,
            skip_group_check=True), reads=[ones_f, ehrow], writes=[psQ], acc=True))
        th.append(lambda p=p: b.op("vector", lambda e: e.tensor_copy(
            EX[:, 2 * p:2 * p + 2], psQ[0:128, 514:516]),
            reads=[psQ], writes=[EX]))
        return th

    psS.w = psA.w; psS.r = list(psA.r)
    psQ.w = psA.w; psQ.r = list(psA.r)

    for p in range(NCH):
        for t in qkv_pair_thunks(p):
            t()
    b.bigfree(*xnT)

    SBb = (psS, psQ)
    psBh = [Reg(Win(psB.h, k * 1024, 1024), f"psBh{k}") for k in range(2)]
    for v in psBh:
        v.w = psB.w
        v.r = list(psB.r)
    PexpV = [Reg(Win(Pexp.h, k * 1024, 1024), f"PexpV{k}") for k in range(4)]
    NPT = 8
    PTq = [b.big(f"PTq_{i}", 1024) for i in range(NPT)]
    RUv = [Reg(Win(RU.h.t, RU.h.c0 + k * 1024, 1024), f"RUv{k}")
           for k in range(2)]
    tile_i = [0]

    def emit_pv(h, qh, kb, pt):
        for qs in range(2):
            b.op("tensor", lambda e, h=h, qh=qh, kb=kb, qs=qs, pt=pt: e.matmul(
                psBh[qh][0:65, qs * 512:(qs + 1) * 512],
                vtok[kb][:, 65 * h:65 * h + 65],
                pt[0:128, qs * 512:(qs + 1) * 512],
                start=(kb == 0), stop=(kb == NTT - 1), skip_group_check=True),
                reads=[vtok[kb], pt], writes=[psBh[qh]], acc=True)

    def build_tail(h, qh):
        qc, r0 = h // 2, 64 * (h % 2)
        rB = 64 - r0
        selR = sel_hi if r0 == 0 else sel_lo
        selU = sel_lo if r0 == 0 else sel_hi
        pb, ruv = psBh[qh], RUv[qh]
        oc = slice(qh * 1024, (qh + 1) * 1024)

        def g_urow():
            b.op("vector", lambda e: e.tensor_scalar(
                urow[...], pb[64:65, 0:1024], float(T), None, ALU.add),
                reads=[pb], writes=[urow])

        def g_recip():
            def _f(e):
                with nc.allow_low_precision(reason="fp16 denom rows"):
                    return e.reciprocal(drec[...], urow[...])
            b.op("vector", _f, reads=[urow], writes=[drec])

        def g_rec2():
            return None

        def g_umul():
            def _f(e):
                with nc.allow_low_precision(reason="fp16 denom rows"):
                    return e.tensor_tensor(urow16[...], pb[64:65, 0:1024],
                                           drec[...], ALU.mult)
            b.op("vector", _f, reads=[pb, drec], writes=[urow16])

        def g_ru():
            X = SBb[(tile_i[0] + 1) % 2]
            for qs in range(2):
                b.op("tensor", lambda e, qs=qs, X=X: e.matmul(
                    X[0:128, qs * 512:(qs + 1) * 512], selR[...],
                    drec[0:1, qs * 512:(qs + 1) * 512], start=True,
                    stop=False), reads=[selR, drec], writes=[X], acc=True)
                b.op("tensor", lambda e, qs=qs, X=X: e.matmul(
                    X[0:128, qs * 512:(qs + 1) * 512], selU[...],
                    urow16[0:1, qs * 512:(qs + 1) * 512], start=False,
                    stop=True), reads=[selU, urow16], writes=[X], acc=True)
            b.op("scalar", lambda e, X=X: e.activation(
                ruv[...], X[0:128, 0:1024], ACT.Copy),
                reads=[X], writes=[ruv])

        def g_om():
            b.op("vector", lambda e: e.tensor_tensor(
                OT[qc][r0:r0 + 64, oc], pb[0:64, 0:1024],
                ruv[rB:rB + 64, 0:1024], ALU.mult),
                reads=[pb, ruv], writes=[OT[qc]])

        def g_os():
            b.op("vector", lambda e: e.scalar_tensor_tensor(
                OT[qc][r0:r0 + 64, oc], ruv[r0:r0 + 64, 0:1024],
                negvA[r0:r0 + 64, qc:qc + 1], OT[qc][r0:r0 + 64, oc],
                ALU.mult, ALU.add),
                reads=[ruv, negvA, OT[qc]], writes=[OT[qc]])
        return [g_urow, g_recip, g_rec2, g_umul, g_ru, g_om, g_os]

    tail_th = []
    for h in range(NH):
        qc, half = h // 2, h % 2
        r0 = 64 * half
        if half == 0:
            if qc > 0:
                b.bigfree(qT[qc - 1], kT[qc - 1])
            OT[qc] = b.big(f"OT_{qc}", T)
        for qh in range(2):
            pend = None
            for kb in range(NTT):
                i = tile_i[0]
                s = SBb[i % 2]
                for qs in range(2):
                    b.op("tensor", lambda e, s=s, qc=qc, r0=r0, kb=kb, qh=qh,
                         qs=qs: e.matmul(
                        s[0:128, qs * 512:(qs + 1) * 512],
                        kT[qc][r0:r0 + 64, kb * 128:(kb + 1) * 128],
                        qT[qc][r0:r0 + 64,
                               qh * 1024 + qs * 512:qh * 1024 + (qs + 1) * 512],
                        start=True, stop=True),
                        reads=[kT[qc], qT[qc]], writes=[s], acc=True)
                if pend is not None:
                    emit_pv(*pend)
                pend = (h, qh, kb, PTq[i % NPT])
                pe = PexpV[i % 4]
                b.op("scalar", lambda e, s=s, pe=pe, h=h: e.activation(
                    pe[...], s[0:128, 0:1024], ACT.Exp, scale=EX[:, h:h + 1]),
                    reads=[s, EX], writes=[pe])
                pt = PTq[i % NPT]
                b.op("vector", lambda e, pe=pe, pt=pt: e.tensor_scalar(
                    pt[...], pe[...], -1.0, None, ALU.add),
                    reads=[pe], writes=[pt])
                tile_i[0] += 1
                if tail_th:
                    tail_th.pop(0)()
            emit_pv(*pend)
            while tail_th:
                tail_th.pop(0)()
            tail_th = build_tail(h, qh)
    while tail_th:
        tail_th.pop(0)()
    b.barrier()
    p4.close()
    b.bigfree(qT[NCH - 1], kT[NCH - 1], *vtok, *PTq, RU)
    for c in range(NCH):
        b.dump(f"OT{c}", OT[c])

    xn1T = [b.big(f"xn1T_{c}", T) for c in range(NCH)]
    Wproj_s = [b.big(f"Wp_{c}", DM) for c in range(NCH)]
    p6 = stack.enter_context(contextlib.ExitStack())
    vpc = b.sb("vpc", [128, 8], F32, scoped=p6)
    srow0 = b.sb("srow0", [1, DM], F32, scoped=p6)
    stat6 = b.sb("stat6", [128, 8], F32, scoped=p6)
    p6w = stack.enter_context(contextlib.ExitStack())
    wpf = b.sb("wpf", [128, DM], F32, scoped=p6w)

    for c in range(NCH):
        b.op("tensor", lambda e, c=c: e.transpose(
            psA[:, 1024 + c:1024 + c + 1], vbp[0:1, c * 128:(c + 1) * 128],
            ident_f[0:1, 0:1]), reads=[vbp, ident_f], writes=[psA], acc=True)
    b.op("vector", lambda e: e.tensor_copy(vpc[...], psA[:, 1024:1032]),
         reads=[psA], writes=[vpc])
    for c in range(NCH):
        b.op("gpsimd", lambda e, c=c: e.dma_start(
            out=Wproj_s[c][...], in_=W_proj[c * 128:(c + 1) * 128, :]),
            writes=[Wproj_s[c]], dma=True)
        b.op("sync", lambda e, c=c: e.dma_start(
            out=wpf[...], in_=W_proj[c * 128:(c + 1) * 128, :]),
            writes=[wpf], dma=True)
        for fc in range(2):
            b.op("tensor", lambda e, c=c, fc=fc: e.matmul(
                psB[0:1, fc * 512:(fc + 1) * 512], vpc[:, c:c + 1],
                wpf[:, fc * 512:(fc + 1) * 512], start=(c == 0), stop=False,
                skip_group_check=True),
                reads=[vpc, wpf], writes=[psB], acc=True)
    b.op("vector", lambda e: e.tensor_tensor(
        srow0[...], psB[0:1, 0:1024], bp_row[...], ALU.add),
        reads=[psB, bp_row], writes=[srow0])
    b.barrier()
    p6w.close()
    g0r = [b.sb(f"g0r_{i}", [128, DM], F32, scoped=p6) for i in range(2)]
    x0r = [b.sb(f"x0r_{i}", [128, DM], F32, scoped=p6) for i in range(2)]
    x1t = [b.sb(f"x1t_{i}", [128, DM], F32, scoped=p6) for i in range(2)]
    tg6 = b.sb("tg6", [128, DM], F32, scoped=p6)
    xn1f = b.sb("xn1f", [128, DM], F32, scoped=p6)
    xsq6 = tg6

    def ln_stats6(xbuf):
        b.op("scalar", lambda e: e.activation(xsq6[...], xbuf[...], ACT.Square),
             reads=[xbuf], writes=[xsq6])
        b.op("vector", lambda e: e.reduce_sum(stat6[:, 0:1], xbuf[...], axis=AX.X),
             reads=[xbuf], writes=[stat6])
        b.op("vector", lambda e: e.reduce_sum(stat6[:, 1:2], xsq6[...], axis=AX.X),
             reads=[xsq6], writes=[stat6])
        b.op("vector", lambda e: e.tensor_scalar(
            stat6[:, 2:3], stat6[:, 0:1], 1.0 / DM, None, ALU.mult),
            reads=[stat6], writes=[stat6])
        b.op("vector", lambda e: e.tensor_tensor(
            stat6[:, 4:5], stat6[:, 2:3], stat6[:, 2:3], ALU.mult),
            reads=[stat6], writes=[stat6])
        b.op("vector", lambda e: e.tensor_scalar(
            stat6[:, 5:6], stat6[:, 1:2], 1.0 / DM, None, ALU.mult),
            reads=[stat6], writes=[stat6])
        b.op("vector", lambda e: e.tensor_tensor(
            stat6[:, 5:6], stat6[:, 5:6], stat6[:, 4:5], ALU.subtract),
            reads=[stat6], writes=[stat6])
        b.op("vector", lambda e: e.tensor_scalar(
            stat6[:, 5:6], stat6[:, 5:6], 1e-5, None, ALU.add),
            reads=[stat6], writes=[stat6])
        b.op("vector", lambda e: e.reciprocal(stat6[:, 6:7], stat6[:, 5:6]),
             reads=[stat6], writes=[stat6])
        b.op("scalar", lambda e: e.activation(
            stat6[:, 3:4], stat6[:, 6:7], ACT.Sqrt), reads=[stat6], writes=[stat6])
        b.op("vector", lambda e: e.tensor_tensor(
            stat6[:, 6:7], stat6[:, 3:4], stat6[:, 3:4], ALU.mult),
            reads=[stat6], writes=[stat6])
        b.op("vector", lambda e: e.tensor_tensor(
            stat6[:, 6:7], stat6[:, 6:7], stat6[:, 5:6], ALU.mult),
            reads=[stat6], writes=[stat6])
        b.op("vector", lambda e: e.tensor_scalar(
            stat6[:, 6:7], stat6[:, 6:7], -0.5, 1.5, ALU.mult, ALU.add),
            reads=[stat6], writes=[stat6])
        b.op("vector", lambda e: e.tensor_tensor(
            stat6[:, 3:4], stat6[:, 3:4], stat6[:, 6:7], ALU.mult),
            reads=[stat6], writes=[stat6])

    for tt in range(NTT):
        ps = psA if tt % 2 == 0 else psB
        gb, xb, x1b = g0r[tt % 2], x0r[tt % 2], x1t[tt % 2]
        b.op("gpsimd", lambda e, gb=gb, tt=tt: e.dma_start(
            out=gb[...], in_=g0_dram[...][tt * 128:(tt + 1) * 128, :]),
            reads=[g0_dram], writes=[gb], dma=True)
        b.op("gpsimd", lambda e, xb=xb, tt=tt: e.dma_start(
            out=xb[...], in_=x_in[tt * 128:(tt + 1) * 128, :]),
            writes=[xb], dma=True)
        for fc in range(2):
            for c in range(NCH):
                b.op("tensor", lambda e, c=c, fc=fc, tt=tt, pm=ps: e.matmul(
                    pm[:, fc * 512:(fc + 1) * 512],
                    OT[c][:, tt * 128:(tt + 1) * 128],
                    Wproj_s[c][:, fc * 512:(fc + 1) * 512],
                    start=(c == 0), stop=False),
                    reads=[OT[c], Wproj_s[c]], writes=[ps], acc=True)
            b.op("tensor", lambda e, fc=fc, pm=ps: e.matmul(
                pm[:, fc * 512:(fc + 1) * 512], ones_f[...],
                srow0[0:1, fc * 512:(fc + 1) * 512], start=False, stop=True),
                reads=[ones_f, srow0], writes=[ps], acc=True)
        b.op("vector", lambda e, gb=gb, pm=ps: e.tensor_tensor(
            tg6[...], pm[:, 0:1024], gb[...], ALU.mult),
            reads=[ps, gb], writes=[tg6])
        b.op("vector", lambda e, xb=xb, x1b=x1b: e.tensor_tensor(
            x1b[...], tg6[...], xb[...], ALU.add),
            reads=[tg6, xb], writes=[x1b])
        b.op("sync", lambda e, x1b=x1b, tt=tt: e.dma_start(
            out=x1_dram[...][tt * 128:(tt + 1) * 128, :], in_=x1b[...]),
            reads=[x1b], writes=[x1_dram], dma=True)
        ln_stats6(x1b)
        b.op("vector", lambda e, x1b=x1b: e.tensor_scalar(
            xn1f[...], x1b[...], stat6[:, 2:3], stat6[:, 3:4],
            ALU.subtract, ALU.mult), reads=[x1b, stat6], writes=[xn1f])
        for c in range(NCH):
            b.op("tensor", lambda e, c=c, pm=ps: e.transpose(
                pm[:, 1024 + c * 128:1024 + c * 128 + 128],
                xn1f[:, c * 128:(c + 1) * 128], ident_f[...]),
                reads=[xn1f, ident_f], writes=[ps], acc=True)
        for c in range(NCH):
            b.op("vector", lambda e, c=c, tt=tt, pm=ps: e.tensor_copy(
                xn1T[c][:, tt * 128:(tt + 1) * 128],
                pm[:, 1024 + c * 128:1024 + c * 128 + 128]),
                reads=[ps], writes=[xn1T[c]])
    b.barrier()
    p6.close()
    b.bigfree(*OT, *Wproj_s)
    for c in range(NCH):
        b.dump(f"xn1T{c}", xn1T[c])

    latentT = [b.big(f"latT_{i}", T) for i in range(2)]
    qrT = [b.big(f"qrT_{i}", T) for i in range(4)]
    krT = b.big("krT", T)
    Wq_s = [b.big(f"Wq_{c}", DM) for c in range(NCH)]
    Wku_s = [b.big(f"Wku_{i}", DM) for i in range(2)]
    Wvu_s = [b.big(f"Wvu_{i}", DM) for i in range(2)]
    cosPs = b.big("cosPs", T)
    sinPs = b.big("sinPs", T)
    p8 = stack.enter_context(contextlib.ExitStack())
    Wqr_s = [b.sb(f"Wqr_{c}", [128, 512], BF16, scoped=p8) for c in range(NCH)]
    Wkr_s = [b.sb(f"Wkr_{c}", [128, DR], BF16, scoped=p8) for c in range(NCH)]
    Wkvd_s = [b.sb(f"Wkvd_{c}", [128, DKV], BF16, scoped=p8) for c in range(NCH)]
    qrRaw = b.sb("qrRaw", [128, T], BF16, scoped=p8)
    qrotbf = b.sb("qrotbf", [128, T], BF16, scoped=p8)
    t81 = b.sb("t81", [128, T], F32, scoped=p8)
    t82 = b.sb("t82", [128, T], F32, scoped=p8)

    for (dst, hi) in ((cosPs, "cosP"), (sinPs, "sinP")):
        b.op("sync", sync_dma(t81[...], DN[hi]), writes=[t81], dma=True)
        b.op("vector", lambda e, dst=dst: e.tensor_copy(dst[...], t81[...]),
             reads=[t81], writes=[dst])
    for c in range(NCH):
        b.op("gpsimd", lambda e, c=c: e.dma_start(
            out=Wq_s[c][...], in_=Wq[c * 128:(c + 1) * 128, :]),
            writes=[Wq_s[c]], dma=True)
        b.op("gpsimd", lambda e, c=c: e.dma_start(
            out=Wqr_s[c][...], in_=Wqr[c * 128:(c + 1) * 128, :]),
            writes=[Wqr_s[c]], dma=True)
        b.op("gpsimd", lambda e, c=c: e.dma_start(
            out=Wkr_s[c][...], in_=Wkr[c * 128:(c + 1) * 128, :]),
            writes=[Wkr_s[c]], dma=True)
        b.op("gpsimd", lambda e, c=c: e.dma_start(
            out=Wkvd_s[c][...], in_=Wkvd[c * 128:(c + 1) * 128, :]),
            writes=[Wkvd_s[c]], dma=True)
    for i in range(2):
        b.op("gpsimd", lambda e, i=i: e.dma_start(
            out=Wku_s[i][...], in_=Wku[i * 128:(i + 1) * 128, :]),
            writes=[Wku_s[i]], dma=True)
        b.op("gpsimd", lambda e, i=i: e.dma_start(
            out=Wvu_s[i][...], in_=Wvu[i * 128:(i + 1) * 128, :]),
            writes=[Wvu_s[i]], dma=True)

    for lc in range(2):
        ps = psA if lc % 2 == 0 else psB
        for dc in range(NCH):
            for qs in range(4):
                b.op("tensor", lambda e, lc=lc, dc=dc, qs=qs, pm=ps: e.matmul(
                    pm[:, qs * 512:(qs + 1) * 512],
                    Wkvd_s[dc][:, lc * 128:(lc + 1) * 128],
                    xn1T[dc][:, qs * 512:(qs + 1) * 512],
                    start=(dc == 0), stop=(dc == NCH - 1)),
                    reads=[Wkvd_s[dc], xn1T[dc]], writes=[ps], acc=True)
        b.op("scalar", lambda e, lc=lc, pm=ps: e.activation(
            latentT[lc][...], pm[...], ACT.Copy),
            reads=[ps], writes=[latentT[lc]])

    def rope(dst, raw, rows, rot_ap, psy):
        for qs in range(4):
            b.op("tensor", lambda e, qs=qs: e.matmul(
                psy[0:rows, qs * 512:(qs + 1) * 512], rot_ap,
                raw[0:rows, qs * 512:(qs + 1) * 512], start=True, stop=True),
                reads=[raw, rot_s], writes=[psy], acc=True)
        b.op("scalar", lambda e: e.activation(
            qrotbf[0:rows, :], psy[0:rows, 0:2048], ACT.Copy),
            reads=[psy], writes=[qrotbf])
        b.op("vector", lambda e: e.tensor_tensor(
            t81[0:rows, :], raw[0:rows, :], cosPs[0:rows, :], ALU.mult),
            reads=[raw, cosPs], writes=[t81])
        b.op("vector", lambda e: e.tensor_tensor(
            t82[0:rows, :], qrotbf[0:rows, :], sinPs[0:rows, :], ALU.mult),
            reads=[qrotbf, sinPs], writes=[t82])
        b.op("vector", lambda e: e.tensor_tensor(
            dst[0:rows, :], t81[0:rows, :], t82[0:rows, :], ALU.add),
            reads=[t81, t82], writes=[dst])

    for qc in range(4):
        ps, psy = (psA, psB) if qc % 2 == 0 else (psB, psA)
        for dc in range(NCH):
            for qs in range(4):
                b.op("tensor", lambda e, qc=qc, dc=dc, qs=qs, pm=ps: e.matmul(
                    pm[:, qs * 512:(qs + 1) * 512],
                    Wqr_s[dc][:, qc * 128:(qc + 1) * 128],
                    xn1T[dc][:, qs * 512:(qs + 1) * 512],
                    start=(dc == 0), stop=(dc == NCH - 1)),
                    reads=[Wqr_s[dc], xn1T[dc]], writes=[ps], acc=True)
        b.op("scalar", lambda e, pm=ps: e.activation(
            qrRaw[...], pm[...], ACT.Copy), reads=[ps], writes=[qrRaw])
        rope(qrT[qc], qrRaw, 128, rot_s[...], psy)
    for dc in range(NCH):
        for qs in range(4):
            b.op("tensor", lambda e, dc=dc, qs=qs: e.matmul(
                psA[0:32, qs * 512:(qs + 1) * 512], Wkr_s[dc][...],
                xn1T[dc][:, qs * 512:(qs + 1) * 512],
                start=(dc == 0), stop=(dc == NCH - 1)),
                reads=[Wkr_s[dc], xn1T[dc]], writes=[psA], acc=True)
    b.op("scalar", lambda e: e.activation(
        qrRaw[0:32, :], psA[0:32, 0:2048], ACT.Copy),
        reads=[psA], writes=[qrRaw])
    rope(krT, qrRaw, 32, rot_s[0:32, 0:32], psB)
    b.barrier()
    p8.close()
    b.bigfree(cosPs, sinPs)
    b.dump("latT0", latentT[0])
    b.dump("qrT0", qrT[0])
    b.dump("krT", krT)

    OT1 = [None] * NCH
    qcat = [b.big(f"qcat_{i}", T) for i in range(4)]
    kcat = [b.big(f"kcat_{i}", T) for i in range(4)]
    v1h = [b.big(f"v1h_{i}", NTT * 65) for i in range(4)]
    PT1q = [b.big(f"PT1q_{i}", 1024) for i in range(4)]
    p9 = stack.enter_context(contextlib.ExitStack())
    mcols = b.sb("mcols", [128, NTT + 4], F32, scoped=p9)
    mcolR = [Reg(Win(mcols.h, c, 1), f"mc{c}") for c in range(NTT)]
    tmpR = [Reg(Win(mcols.h, NTT + c, 1), f"mt{c}") for c in range(4)]
    cem2 = [b.sb(f"cem_{i}", [1, T], F32, scoped=p9) for i in range(2)]
    Dsb = b.sb("Dsb", [1, T], F32, scoped=p9)
    drec1 = b.sb("drec1", [1, T], BF16, scoped=p9)
    lnr1 = b.sb("lnr1", [1, 1024], F32, scoped=p9)
    Rsb1 = b.sb("Rsb1", [64, T], BF16, scoped=p9)
    psAh = psA.h
    S1b = [Reg(Win(psAh, k * 512, 512), f"S1q{k}") for k in range(4)]
    for v in S1b:
        v.w = psA.w
        v.r = list(psA.r)

    for i in range(4):
        b.op("gpsimd", lambda e, i=i: e.memset(
            v1h[i][...].rearrange("p (t j) -> p t j", j=65)[:, :, 64:65], 1.0),
            writes=[v1h[i]])
    for i in (0, 2):
        b.op("gpsimd", lambda e, i=i: e.memset(kcat[i][96:97, :], -1.0),
             writes=[kcat[i]])
        b.op("sync", lambda e, i=i: e.dma_start(
            out=kcat[i][64:96, :], in_=krT[0:32, :]),
            reads=[krT], writes=[kcat[i]], dma=True)
    b.bigfree(krT)

    def build_units(h):
        qlo, qhi = qcat[h % 4], qcat[(h + 1) % 4]
        klo, khi = kcat[h % 4], kcat[(h + 1) % 4]
        th = []

        def g_q(qs, h=h):
            i = tile1_i[0]
            tile1_i[0] += 1
            X = S1b[i % 4]
            for dc in range(NCH):
                b.op("tensor", lambda e, dc=dc, qs=qs, h=h, X=X: e.matmul(
                    X[0:128, 0:512],
                    Wq_s[dc][:, 64 * h:64 * h + 128],
                    xn1T[dc][:, qs * 512:(qs + 1) * 512],
                    start=(dc == 0), stop=(dc == NCH - 1)),
                    reads=[Wq_s[dc], xn1T[dc]], writes=[X], acc=True)
            b.op("scalar", lambda e, qs=qs, X=X: e.activation(
                qlo[0:64, qs * 512:qs * 512 + 512],
                X[0:64, 0:512], ACT.Copy), reads=[X], writes=[qlo])
            b.op("scalar", lambda e, qs=qs, X=X: e.activation(
                qhi[64:128, qs * 512:qs * 512 + 512],
                X[64:128, 0:512], ACT.Copy), reads=[X], writes=[qhi])
        th += [lambda qs=qs: g_q(qs) for qs in range(4)]
        th.append(lambda: b.op("gpsimd", lambda e: e.dma_start(
            out=qhi[0:64, :], in_=qhi[64:128, :]),
            reads=[qhi], writes=[qhi], dma=True))
        for dh in range(2):
            qb_ = qlo if dh == 0 else qhi
            th.append(lambda h=h + dh, qb_=qb_: b.op(
                "sync", lambda e: e.dma_start(
                    out=qb_[64:96, :],
                    in_=qrT[h // 4][32 * (h % 4):32 * (h % 4) + 32, :]),
                reads=[qrT[h // 4]], writes=[qb_], dma=True))

        def g_k(qs, h=h):
            i = tile1_i[0]
            tile1_i[0] += 1
            X = S1b[i % 4]
            for lc in range(2):
                b.op("tensor", lambda e, lc=lc, qs=qs, h=h, X=X: e.matmul(
                    X[0:128, 0:512],
                    Wku_s[lc][:, 64 * h:64 * h + 128],
                    latentT[lc][:, qs * 512:(qs + 1) * 512],
                    start=(lc == 0), stop=(lc == 1)),
                    reads=[Wku_s[lc], latentT[lc]], writes=[X], acc=True)
            b.op("scalar", lambda e, qs=qs, X=X: e.activation(
                klo[0:64, qs * 512:qs * 512 + 512],
                X[0:64, 0:512], ACT.Copy), reads=[X], writes=[klo])
            b.op("scalar", lambda e, qs=qs, X=X: e.activation(
                khi[64:128, qs * 512:qs * 512 + 512],
                X[64:128, 0:512], ACT.Copy), reads=[X], writes=[khi])
        th += [lambda qs=qs: g_k(qs) for qs in range(4)]
        th.append(lambda: b.op("gpsimd", lambda e: e.dma_start(
            out=khi[0:64, :], in_=khi[64:128, :]),
            reads=[khi], writes=[khi], dma=True))
        th.append(lambda: b.op("gpsimd", lambda e: e.dma_start(
            out=khi[64:96, :], in_=klo[64:96, :]),
            reads=[klo], writes=[khi], dma=True))
        th.append(lambda: b.op("gpsimd", lambda e: e.memset(
            khi[96:97, :], -1.0), writes=[khi]))

        def g_v(dh, vg, h=h):
            vb_ = v1h[(h + dh) % 4]
            i = tile1_i[0]
            tile1_i[0] += 1
            X = S1b[i % 4]
            for tt in range(8 * vg, 8 * vg + 8):
                for lc in range(2):
                    b.op("tensor", lambda e, tt=tt, lc=lc, hh=h + dh, X=X:
                         e.matmul(
                        X[:, (tt % 8) * 64:(tt % 8) * 64 + 64],
                        latentT[lc][:, tt * 128:(tt + 1) * 128],
                        Wvu_s[lc][:, 64 * hh:64 * hh + 64],
                        start=(lc == 0), stop=(lc == 1)),
                        reads=[latentT[lc], Wvu_s[lc]], writes=[X], acc=True)
            b.op("scalar", lambda e, vg=vg, vb_=vb_, X=X: e.activation(
                vb_[...].rearrange("p (t j) -> p t j", j=65)
                [:, 8 * vg:8 * vg + 8, 0:64],
                X[:, 0:512].rearrange("p (t j) -> p t j", j=64), ACT.Copy),
                reads=[X], writes=[vb_])
        for dh in range(2):
            for vg in range(2):
                th.append(lambda dh=dh, vg=vg: g_v(dh, vg))
        return th

    tile1_i = [0]
    pump_q = []

    def emit_pv1(h, kb, a, bnd, pt):
        vb_ = v1h[h % 4]
        c = a
        while c < bnd:
            ce = min((c // 512 + 1) * 512, bnd)
            b.op("tensor", lambda e, kb=kb, c=c, ce=ce, a=a, pt=pt,
                 vb_=vb_: e.matmul(
                psB[0:65, c:ce], vb_[:, 65 * kb:65 * kb + 65],
                pt[0:128, c - a:ce - a], start=(kb == 0),
                stop=(kb == 4 * (c // 512) + 3), skip_group_check=True),
                reads=[vb_, pt], writes=[psB], acc=True)
            c = ce

    def do_p1(h, qb, a, bnd):
        qb_, kb_ = qcat[h % 4], kcat[h % 4]
        i = tile1_i[0]
        tile1_i[0] += 1
        X = S1b[i % 4]
        b.op("tensor", lambda e, qb=qb, a=a, bnd=bnd, X=X,
             qb_=qb_, kb_=kb_: e.matmul(
            X[0:128, 0:bnd - a],
            qb_[0:96, qb * 128:(qb + 1) * 128],
            kb_[0:96, a:bnd], start=True, stop=True),
            reads=[qb_, kb_], writes=[X], acc=True)
        d0 = qb * 128
        if a <= d0 < bnd:
            b.op("vector", lambda e, d0=d0, a=a, X=X: e.tensor_tensor(
                X[0:128, d0 - a:d0 - a + 128],
                X[0:128, d0 - a:d0 - a + 128], caus_add_s[...],
                ALU.add), reads=[X, caus_add_s], writes=[X])
        if a == 0:
            b.op("vector", lambda e, qb=qb, bnd=bnd, X=X: e.reduce_max(
                mcols[:, qb:qb + 1], X[0:128, 0:bnd], axis=AX.X),
                reads=[X], writes=[mcolR[qb]])
        else:
            tm = tmpR[i % 4]
            b.op("vector", lambda e, bnd=bnd, a=a, X=X, tm=tm: e.reduce_max(
                tm[0:128, 0:1], X[0:128, 0:bnd - a], axis=AX.X),
                reads=[X], writes=[tm])
            b.op("vector", lambda e, qb=qb, tm=tm: e.tensor_tensor(
                mcols[:, qb:qb + 1], mcols[:, qb:qb + 1],
                tm[0:128, 0:1], ALU.max),
                reads=[tm, mcolR[qb]], writes=[mcolR[qb]])

    def p1_units(h):
        for qb in range(NTT):
            kw = (qb + 1) * 128
            a = 0
            while a < kw:
                bnd = min(a + 512, kw)
                yield (qb, a, bnd)
                a = bnd

    def mrow(h):
        qb_ = qcat[h % 4]
        cemX = cem2[h % 2]
        for qb in range(NTT):
            Xm = S1b[qb // 4]
            b.op("tensor", lambda e, qb=qb: e.transpose(
                psAh[0:1, qb * 128:(qb + 1) * 128], mcols[:, qb:qb + 1],
                ident_f[...]), reads=[mcolR[qb], ident_f], writes=[Xm],
                acc=True)
        b.op("scalar", lambda e, qb_=qb_: e.activation(
            qb_[96:97, :], psAh[0:1, 0:2048], ACT.Copy),
            reads=S1b, writes=[qb_])
        b.op("vector", lambda e, qb_=qb_: e.tensor_tensor(
            Dsb[...], psAh[0:1, 0:2048], qb_[96:97, :], ALU.subtract),
            reads=S1b + [qb_], writes=[Dsb])
        b.op("scalar", lambda e, h=h, cemX=cemX: e.activation(
            cemX[...], Dsb[...], ACT.Exp, scale=L1SCALE,
            bias=lobo_row[0:1, h:h + 1]),
            reads=[Dsb, lobo_row], writes=[cemX])

    def tail_half(h, k):
        hc, hr = h // 2, 64 * (h % 2)
        cemX = cem2[h % 2]
        ks = slice(k * 512, (k + 1) * 512)
        b.op("vector", lambda e: e.tensor_tensor(
            Dsb[0:1, ks], psB[64:65, ks], cemX[0:1, ks], ALU.add),
            reads=[psB, cemX], writes=[Dsb])
        b.op("scalar", lambda e: e.activation(
            lnr1[0:1, 0:512], Dsb[0:1, ks], ACT.Ln), reads=[Dsb],
            writes=[lnr1])
        b.op("scalar", lambda e: e.activation(
            drec1[0:1, ks], lnr1[0:1, 0:512], ACT.Exp, scale=-1.0),
            reads=[lnr1], writes=[drec1])
        X = S1b[(tile1_i[0] + 1) % 4]
        b.op("tensor", lambda e, X=X: e.matmul(
            X[0:64, 0:512], sel_lo[0:1, 0:64], drec1[0:1, ks],
            start=True, stop=True),
            reads=[sel_lo, drec1], writes=[X], acc=True)
        b.op("scalar", lambda e, X=X: e.activation(
            Rsb1[0:64, ks], X[0:64, 0:512], ACT.Copy),
            reads=[X], writes=[Rsb1])
        b.op("vector", lambda e: e.tensor_tensor(
            OT1[hc][hr:hr + 64, ks], psB[0:64, ks], Rsb1[0:64, ks],
            ALU.mult), reads=[psB, Rsb1], writes=[OT1[hc]])

    def merged_step(h):
        qb_, kb_ = qcat[h % 4], kcat[h % 4]
        g1 = p1_units(h + 1) if h + 1 < NH else iter(())
        pend = None
        for qtile in range(4):
            if h > 0:
                tail_half(h - 1, qtile)
            for kb in range(min(4 * qtile + 4, NTT)):
                a = max(kb * 128, qtile * 512)
                bnd = (qtile + 1) * 512
                i = tile1_i[0]
                tile1_i[0] += 1
                X = S1b[i % 4]
                pt = PT1q[i % 4]
                b.op("tensor", lambda e, kb=kb, a=a, bnd=bnd, qtile=qtile,
                     X=X, qb_=qb_, kb_=kb_: e.matmul(
                    X[0:128, a - qtile * 512:bnd - qtile * 512],
                    kb_[0:97, kb * 128:(kb + 1) * 128],
                    qb_[0:97, a:bnd], start=True, stop=True),
                    reads=[kb_, qb_], writes=[X], acc=True)
                if pend is not None:
                    emit_pv1(*pend)
                pend = (h, kb, a, bnd, pt)
                b.op("scalar", lambda e, a=a, qtile=qtile, bnd=bnd, X=X,
                     pt=pt: e.activation(
                    pt[0:128, 0:bnd - a],
                    X[0:128, a - qtile * 512:bnd - qtile * 512],
                    ACT.Exp, scale=L1SCALE), reads=[X], writes=[pt])
                if a == kb * 128:
                    b.op("vector", lambda e, pt=pt: e.tensor_tensor(
                        pt[0:128, 0:128], pt[0:128, 0:128], caus_mul_s[...],
                        ALU.mult), reads=[pt, caus_mul_s], writes=[pt])
                t1 = next(g1, None)
                if t1 is not None:
                    do_p1(h + 1, *t1)
                if pump_q:
                    pump_q.pop(0)()
        emit_pv1(*pend)
        for t1 in g1:
            do_p1(h + 1, *t1)
            if pump_q:
                pump_q.pop(0)()
        while pump_q:
            pump_q.pop(0)()
        if h + 1 < NH:
            mrow(h + 1)

    for t in build_units(0):
        t()
    pump_q = build_units(2)
    for u in p1_units(0):
        do_p1(0, *u)
        if pump_q:
            pump_q.pop(0)()
    mrow(0)
    for h in range(NH):
        if h in (1, 5, 9):
            b.bigfree(qrT[(h - 1) // 4])
        if h == 13:
            b.bigfree(qrT[3], *latentT, *Wku_s, *Wvu_s)
        if h % 2 == 0:
            OT1[h // 2] = b.big(f"OT1_{h // 2}", T)
        merged_step(h)
        if h % 2 == 1 and h + 3 < NH:
            pump_q += build_units(h + 3)
    for k in range(4):
        tail_half(NH - 1, k)
    b.barrier()
    p9.close()
    b.bigfree(*Wq_s, *qcat, *kcat, *v1h, *PT1q)
    for c in range(NCH):
        b.dump(f"OT1_{c}", OT1[c])

    xkT = [b.big(f"xkT_{c}", T) for c in range(NCH)]
    xrT = [b.big(f"xrT_{c}", T) for c in range(NCH)]
    p10m = stack.enter_context(contextlib.ExitStack())
    td8 = b.sb("td8", [128, T], BF16, scoped=p10m)
    for c in range(NCH):
        for (dstl, mc) in ((xkT, 8 + c), (xrT, c)):
            dst = dstl[c]
            b.op("vector", lambda e, c=c, dst=dst, mc=mc: e.tensor_scalar(
                dst[0:128, 0:1], xn1T[c][:, 0:1], mu_col[:, mc:mc + 1], None,
                ALU.mult), reads=[xn1T[c], mu_col], writes=[dst])
            b.op("vector", lambda e, c=c: e.tensor_tensor(
                td8[:, 0:2047], xn1T[c][:, 1:2048], xn1T[c][:, 0:2047],
                ALU.subtract), reads=[xn1T[c]], writes=[td8])
            b.op("vector", lambda e, c=c, dst=dst, mc=mc: e.scalar_tensor_tensor(
                dst[0:128, 1:2048], td8[:, 0:2047], mu_col[:, mc:mc + 1],
                xn1T[c][:, 0:2047], ALU.mult, ALU.add),
                reads=[td8, mu_col, xn1T[c]], writes=[dst])
    b.barrier()
    p10m.close()
    b.bigfree(*xn1T)

    kactT = [b.big(f"kactT_{c}", T) for c in range(NCH)]
    p10k = stack.enter_context(contextlib.ExitStack())
    WkT_s = [b.sb(f"WkT_{c}", [128, DM], BF16, scoped=p10k) for c in range(NCH)]
    ktmp = b.sb("ktmp", [128, T], F32, scoped=p10k)
    for cc in range(NCH):
        b.op("gpsimd", lambda e, cc=cc: e.dma_start(
            out=WkT_s[cc][...],
            in_=wT_dram[...][1][cc * 128:(cc + 1) * 128, :]),
            reads=[wT_dram], writes=[WkT_s[cc]], dma=True)
    for do in range(NCH):
        ps = psA if do % 2 == 0 else psB
        for cc in range(NCH):
            for qs in range(4):
                b.op("tensor", lambda e, do=do, cc=cc, qs=qs, pm=ps: e.matmul(
                    pm[:, qs * 512:(qs + 1) * 512],
                    WkT_s[cc][:, do * 128:(do + 1) * 128],
                    xkT[cc][:, qs * 512:(qs + 1) * 512],
                    start=(cc == 0), stop=(cc == NCH - 1)),
                    reads=[WkT_s[cc], xkT[cc]], writes=[ps], acc=True)
        b.op("scalar", lambda e, pm=ps: e.activation(
            ktmp[...], pm[...], ACT.Relu), reads=[ps], writes=[ktmp])
        b.op("scalar", lambda e, do=do: e.activation(
            kactT[do][...], ktmp[...], ACT.Square),
            reads=[ktmp], writes=[kactT[do]])
    b.barrier()
    p10k.close()
    b.bigfree(*xkT)

    WrT_s = [b.big(f"WrT_{c}", DM) for c in range(NCH)]
    WvT_s = [b.big(f"WvT_{c}", DM) for c in range(NCH)]
    Wo_s = [b.big(f"Wo_{c}", DM) for c in range(NCH)]
    p10 = stack.enter_context(contextlib.ExitStack())
    rtt = b.sb("rtt", [128, DM], F32, scoped=p10)
    tkv = b.sb("tkv", [128, DM], F32, scoped=p10)
    g1t = b.sb("g1t", [128, DM], F32, scoped=p10)
    tg1 = b.sb("tg1", [128, DM], F32, scoped=p10)
    x1r = [b.sb(f"x1r_{i}", [128, DM], F32, scoped=p10) for i in range(2)]
    x2t = [b.sb(f"x2t_{i}", [128, DM], F32, scoped=p10) for i in range(2)]
    for cc in range(NCH):
        b.op("gpsimd", lambda e, cc=cc: e.dma_start(
            out=WrT_s[cc][...],
            in_=wT_dram[...][0][cc * 128:(cc + 1) * 128, :]),
            reads=[wT_dram], writes=[WrT_s[cc]], dma=True)
        b.op("gpsimd", lambda e, cc=cc: e.dma_start(
            out=WvT_s[cc][...],
            in_=wT_dram[...][2][cc * 128:(cc + 1) * 128, :]),
            reads=[wT_dram], writes=[WvT_s[cc]], dma=True)
        b.op("gpsimd", lambda e, cc=cc: e.dma_start(
            out=Wo_s[cc][...], in_=Wo[cc * 128:(cc + 1) * 128, :]),
            writes=[Wo_s[cc]], dma=True)
    pAr = Reg(Win(psA.h, 0, 1024), "pAr")
    pAkv = Reg(Win(psA.h, 1024, 1024), "pAkv")
    for v in (pAr, pAkv):
        v.w = psA.w
        v.r = list(psA.r)
    for tt in range(NTT):
        xrb = x1r[tt % 2]
        x2b = x2t[tt % 2]
        b.op("gpsimd", lambda e, xrb=xrb, tt=tt: e.dma_start(
            out=xrb[...], in_=x1_dram[...][tt * 128:(tt + 1) * 128, :]),
            reads=[x1_dram], writes=[xrb], dma=True)
        for fc in range(2):
            for cc in range(NCH):
                b.op("tensor", lambda e, fc=fc, cc=cc, tt=tt: e.matmul(
                    psA[:, fc * 512:(fc + 1) * 512],
                    xrT[cc][:, tt * 128:(tt + 1) * 128],
                    WrT_s[cc][:, fc * 512:(fc + 1) * 512],
                    start=(cc == 0), stop=(cc == NCH - 1)),
                    reads=[xrT[cc], WrT_s[cc]], writes=[pAr], acc=True)
        for fc in range(2):
            for cc in range(NCH):
                b.op("tensor", lambda e, fc=fc, cc=cc, tt=tt: e.matmul(
                    psA[:, 1024 + fc * 512:1024 + (fc + 1) * 512],
                    kactT[cc][:, tt * 128:(tt + 1) * 128],
                    WvT_s[cc][:, fc * 512:(fc + 1) * 512],
                    start=(cc == 0), stop=(cc == NCH - 1)),
                    reads=[kactT[cc], WvT_s[cc]], writes=[pAkv], acc=True)
        for fc in range(2):
            for cc in range(NCH):
                b.op("tensor", lambda e, fc=fc, cc=cc, tt=tt: e.matmul(
                    psB[:, fc * 512:(fc + 1) * 512],
                    OT1[cc][:, tt * 128:(tt + 1) * 128],
                    Wo_s[cc][:, fc * 512:(fc + 1) * 512],
                    start=(cc == 0), stop=(cc == NCH - 1)),
                    reads=[OT1[cc], Wo_s[cc]], writes=[psB], acc=True)
        b.op("scalar", lambda e: e.activation(
            rtt[...], psA[:, 0:1024], ACT.Sigmoid), reads=[pAr], writes=[rtt])
        b.op("vector", lambda e: e.tensor_tensor(
            tkv[...], psA[:, 1024:2048], rtt[...], ALU.mult),
            reads=[pAkv, rtt], writes=[tkv])
        b.op("scalar", lambda e: e.activation(
            g1t[...], tkv[...], ACT.Silu), reads=[tkv], writes=[g1t])
        b.op("vector", lambda e: e.tensor_tensor(
            tg1[...], psB[:, 0:1024], g1t[...], ALU.mult),
            reads=[psB, g1t], writes=[tg1])
        b.op("vector", lambda e, xrb=xrb, x2b=x2b: e.tensor_tensor(
            x2b[...], tg1[...], xrb[...], ALU.add),
            reads=[tg1, xrb], writes=[x2b])
        b.op("sync", lambda e, x2b=x2b, tt=tt: e.dma_start(
            out=y_out[tt * 128:(tt + 1) * 128, :], in_=x2b[...]),
            reads=[x2b], writes=[y_reg], dma=True)
    for i in range(NLANE):
        key = f"sync_d{i}"
        b._wait("sync", key, b.esem[key][1])
    b.barrier()

    return nc, b, DN, y_reg, (g0_dram, x1_dram, wT_dram), dict(locals())



_IN_MAP_NAMES = {
    "W_qkv": "L0_attn_W_qkv", "b_qkv": "L0_attn_b_qkv",
    "W_proj": "L0_attn_W_proj", "b_proj": "L0_attn_b_proj",
    "fbias": "L0_ffn_bias",
    "Wq": "L1_attn_Wq", "Wqr": "L1_attn_Wqr", "Wkr": "L1_attn_Wkr",
    "Wkvd": "L1_attn_Wkvd", "Wku": "L1_attn_Wku", "Wvu": "L1_attn_Wvu",
    "Wo": "L1_attn_Wo", "lobo": "L1_attn_lobo_log",
    "Wr": "L1_ffn_W_r", "Wk": "L1_ffn_W_k", "Wv": "L1_ffn_W_v",
    "mu_r": "L1_ffn_mu_r", "mu_k": "L1_ffn_mu_k",
}


def make_in_maps(inputs):
    consts = host_consts()
    x = np.ascontiguousarray(np.asarray(inputs["L0_attn_x"], dtype=np.float32))
    base = {k: np.ascontiguousarray(np.asarray(inputs[v], np.float32))
            for k, v in _IN_MAP_NAMES.items()}
    base["tern"] = np.ascontiguousarray(
        np.asarray(inputs["L0_ffn_ternary_weight"], np.int32))
    base["ascale"] = np.asarray(
        inputs["L0_ffn_act_scale"], np.float32).reshape([1])
    base.update(consts)
    return [dict(base, x=x[core]) for core in range(8)]


def kernel(**inputs):
    with contextlib.ExitStack() as stack:
        nc, b, DN, y_reg, scratch, env = build_nc(stack)
        b.finalize()
        in_maps = make_in_maps(inputs)
        res = run_bass_kernel_spmd(nc, in_maps, core_ids=list(range(8)))
    return np.stack([r["y"] for r in res.results], axis=0)
